# version 23
# baseline (speedup 1.0000x reference)
import numpy as np

import concourse.bass as bass
import concourse.mybir as mybir
from concourse.bass_utils import run_bass_kernel_spmd

F32 = mybir.dt.float32
BF16 = mybir.dt.bfloat16
AF = mybir.ActivationFunctionType
ALU = mybir.AluOpType
AX = mybir.AxisListType

D = 1024
H = 8
DH = 128
F = 2048
DEPTH = 4
EPS = 1e-5
FC = D // 128
HC = F // 128
CH = 256
NST = CH // 128

GAMMAS = 1.0 - np.exp(np.linspace(np.log(1 / 32), np.log(1 / 512), H))


def _gp(h, e):
    v = float(GAMMAS[h]) ** e
    return 0.0 if abs(v) < 1e-37 else v


def host_constants():
    c = {}
    p = np.arange(128, dtype=np.float64)
    gq = np.zeros((128, NST * 1024), dtype=np.float32)
    gka = np.zeros((128, NST * 1024), dtype=np.float32)
    for st in range(NST):
        for h in range(H):
            rr = st * 128 + p
            gq[:, st * 1024 + h * 128:st * 1024 + (h + 1) * 128] = \
                (GAMMAS[h] ** (rr + 1)).astype(np.float32)[:, None]
            gka[:, st * 1024 + h * 128:st * 1024 + (h + 1) * 128] = \
                (GAMMAS[h] ** (-(rr + 1))).astype(np.float32)[:, None]
    c["gq_t"] = gq
    c["gka_t"] = gka
    gsf = np.zeros((128, NST * 1024), dtype=np.float32)
    for st in range(NST):
        for h in range(H):
            col = (GAMMAS[h] ** (CH - 1 - (st * 128 + p))).astype(np.float32)
            gsf[:, st * 1024 + h * 128:st * 1024 + (h + 1) * 128] = col[:, None]
    c["gsf_t"] = gsf
    m = np.arange(128)
    c["tri"] = (m[None, :] >= m[:, None]).astype(np.float32)
    c["ident"] = np.eye(128, dtype=np.float32)
    c["ones1"] = np.ones((1, 128), dtype=np.float32)
    hs = np.zeros((128, H * 8), dtype=np.float32)
    for h in range(H):
        hs[:, h * 8 + h] = 1.0
    c["head_sel"] = hs
    s8 = np.zeros((8, H * 128), dtype=np.float32)
    for h in range(H):
        s8[h, h * 128:(h + 1) * 128] = 1.0
    c["sel8"] = s8
    return c


def rope_tables(pos0, T):
    inv = 10000.0 ** (-np.arange(0, DH, 2, dtype=np.float64) / DH)
    pos = np.arange(pos0, pos0 + T, dtype=np.float64)
    ang = pos[:, None] * inv[None, :]
    return np.cos(ang).astype(np.float32), np.sin(ang).astype(np.float32)


ENGS = ("sync", "scalar", "vector", "tensor", "gpsimd")


class B:
    def __init__(self, nc):
        self.nc = nc
        self.stream = {e: [] for e in ENGS}
        self.sems = {}
        self.cnt = {}
        self.bank_w = {}
        self.bank_r = {}
        self.lastd = {}
        self.waited = {e: {} for e in ENGS}
        self.pos = {e: 0 for e in ENGS}
        self.prod = {}

    def sem(self, n):
        if n not in self.sems:
            self.sems[n] = self.nc.alloc_semaphore(name=n)
            self.cnt[n] = 0
        return self.sems[n]

    def _flat(self, deps, out=None):
        if out is None:
            out = []
        for t in deps:
            if t is None:
                continue
            if isinstance(t, list):
                self._flat(t, out)
            else:
                out.append(t)
        return out

    def _filter_waits(self, eng, dd):
        ws = []
        n = self.pos[eng]
        for s, v in dd.items():
            if v <= 0:
                continue
            if v <= self.waited[eng].get(s, 0):
                continue
            pr = self.prod.get(s)
            if pr is not None and pr[0] == "tensor" and eng == "tensor":
                continue
            ws.append((s, v))
        return ws

    def emit(self, eng, build, deps=(), inc=None, by=1):
        dd = {}
        for s, v in self._flat(deps):
            dd[s] = max(dd.get(s, 0), v)
        ws_names = self._filter_waits(eng, dd)
        for s, v in ws_names:
            self.waited[eng][s] = v
        ws = [(self.sem(s), v) for s, v in ws_names]
        tok = None
        it = None
        my_pos = self.pos[eng] + len(ws)
        if inc is not None:
            self.sem(inc)
            self.cnt[inc] += by
            tok = (inc, self.cnt[inc])
            it = (self.sems[inc], by)
            pr = self.prod.get(inc)
            if pr is None:
                self.prod[inc] = (eng, {self.cnt[inc]: my_pos})
            elif pr[0] == eng:
                pr[1][self.cnt[inc]] = my_pos
            else:
                self.prod[inc] = (None, {})

        def run(e, ws=ws, build=build, it=it):
            for s, v in ws:
                e.wait_ge(s, v)
            inst = build(e)
            if it is not None:
                inst.then_inc(it[0], it[1])

        self.stream[eng].append(run)
        self.pos[eng] = my_pos + 1
        return tok

    def wait(self, eng, *deps):
        dd = {}
        for s, v in self._flat(list(deps)):
            dd[s] = max(dd.get(s, 0), v)
        ws_names = self._filter_waits(eng, dd)
        for s, v in ws_names:
            self.waited[eng][s] = v
        ws = [(self.sem(s), v) for s, v in ws_names]

        def run(e, ws=ws):
            for s, v in ws:
                e.wait_ge(s, v)

        self.stream[eng].append(run)
        self.pos[eng] += len(ws)

    def bw(self, bk):
        return [self.bank_w.get(bk)] + list(self.bank_r.get(bk, []))

    def set_w(self, bk, tok):
        self.bank_w[bk] = tok
        self.bank_r[bk] = []

    def add_r(self, bk, tok):
        self.bank_r.setdefault(bk, []).append(tok)

    def last(self, key):
        return self.lastd.get(key)

    def note(self, key, tok):
        self.lastd[key] = tok

    def note_add(self, key, tok):
        cur = self.lastd.get(key)
        if cur is None:
            self.lastd[key] = [tok]
        elif isinstance(cur, list):
            cur.append(tok)
        else:
            self.lastd[key] = [cur, tok]


def build_program(T, depth=DEPTH, n_cores=8, collective=True,
                  seq_split=False):
    assert T % CH == 0
    NCH = T // CH
    NT = T // 128
    nc = bass.Bass("TRN2", target_bir_lowering=False)
    b = B(nc)

    dr = {}

    def dram(name, shape, dtype=F32, kind="ExternalInput"):
        dr[name] = nc.dram_tensor(name, shape, dtype, kind=kind)

    dram("x_in", [T, D])
    dram("out", [T, D], kind="ExternalOutput")
    for li in range(depth):
        for w, shp in (("wq", [D, D]), ("wk", [D, D]), ("wv", [D, D]),
                       ("wg", [D, D]), ("wo", [D, D]), ("w1", [D, F]),
                       ("w2", [F, D])):
            dram(f"{w}{li}", shp)
    dram("fw", [D, D])
    dram("fbias", [128, D])
    dram("ln_sb", [128, 2 * FC])
    dram("gn_sb", [128, depth * 2 * FC])
    dram("rope_c", [T, 64])
    dram("rope_s", [T, 64])
    dram("gq_t", [128, NST * 1024])
    dram("gka_t", [128, NST * 1024])
    dram("gsf_t", [128, NST * 1024])
    dram("tri", [128, 128])
    dram("ident", [128, 128])
    dram("ones1", [1, 128])
    dram("sel8", [8, H * 128])
    dram("head_sel", [128, H * 8])
    dram("xb0", [T, D], kind="Internal")
    dram("xb1", [T, D], kind="Internal")
    dram("xb2", [T, D], kind="Internal")
    dram("bmax_i", [128, 4], kind="Internal")
    dram("bmax_o", [128, 4], kind="Internal")
    dram("kvk", [(T // CH) * 128, H * CH], kind="Internal")
    dram("kvv", [(T // CH) * 128, NST * 1024], kind="Internal")
    dram("snapb", [(T // CH) * 128, H * DH], kind="Internal")
    dram("bst_i", [128, H * DH], kind="Internal")
    dram("bst_o", [128, H * DH], kind="Internal")
    dram("cmask", [128, 2])
    wsp_keys = []
    for li_ in range(depth):
        for nm in ("wk", "wv", "wq", "wg"):
            for k_ in range(FC):
                wsp_keys.append((f"{nm}{li_}", k_, None))
        for k_ in range(FC):
            wsp_keys.append((f"wo{li_}", k_, None))
        for hf_ in range(2):
            for k_ in range(FC):
                wsp_keys.append((f"w1{li_}", k_, hf_))
        for k_ in range(F // 128):
            wsp_keys.append((f"w2{li_}", k_, None))
    wsp_row = {key: i * 128 for i, key in enumerate(wsp_keys)}
    QW_ROW = len(wsp_keys) * 128
    dram("wsp", [(len(wsp_keys) + 4) * 128, 2048], BF16, kind="Internal")

    sb = {}

    def sball(name, shape, dtype=F32):
        sb[name] = nc.alloc_sbuf_tensor("s_" + name, shape, dtype)

    sball("xchunk", [128, 2 * NST * 1024])
    sball("lnt", [128, NST * 1024])
    sball("lnf", [128, FC * CH])
    sball("kf", [128, H * CH])
    sball("ktm", [128, H * NST * 128])
    sball("vf", [128, NST * 1024])
    sball("qf", [128, H * CH])
    sball("gfb", [128, H * CH])
    sball("ysb", [128, H * CH])
    sball("attm", [128, NST * (NST + 1) * 64])
    sball("attm2", [128, NST * (NST + 1) * 64])
    sball("gyf", [128, H * CH])
    sball("yr", [128, NST * 1024])
    sball("ln2f_h", [128, FC * CH], BF16)
    sball("ln2f_l", [128, FC * CH], BF16)
    sball("hf_h", [128, HC * CH], BF16)
    sball("hf_l", [128, HC * CH], BF16)
    sball("stats", [128, 2 * NT])
    sball("st2", [128, 2 * NST])
    sball("sraw", [128, 8])
    sball("small", [128, 8])
    sball("sst", [128, H * DH])
    sball("wpool", [128, 4 * 1024])
    sball("rope2", [128, 2 * 2 * NST * 64])
    sball("gq", [128, NST * 1024])
    sball("gka", [128, NST * 1024])
    sball("gsf", [128, NST * 1024])
    sball("tri", [128, 128])
    sball("ident", [128, 128])
    sball("identb", [128, 128], BF16)
    sball("ones1", [1, 128])
    sball("sel8", [8, H * 128])
    sball("head_sel", [128, H * 8])
    sball("lnsb", [128, 2 * FC])
    sball("gnsb", [128, depth * 2 * FC])
    sball("fsc", [128, 1024])
    sball("bias2", [128, D])
    sball("qwt", [128, FC * 1024], BF16)
    sball("qxf", [128, 2 * FC * 128], BF16)
    sball("qbf", [128, 1024], BF16)
    sball("maxc", [128, 1])
    sball("gmaxc", [128, 4])
    sball("wsc", [128, 1])
    sball("thc", [128, 2])
    sball("qsc", [128, 1])
    sball("sic", [128, 1])
    sball("srow", [1, 4])
    sball("epsc", [128, 1])
    sball("cmask", [128, 2])
    sb["qtm"] = sb["ysb"].ap()
    sb["ropescr"] = sb["hf_h"].ap().bitcast(F32)[:, 0:2048]
    sb["gstat"] = sb["hf_l"].ap().bitcast(F32)[0:8, 0:1024]
    sb["lnf_h"] = sb["qwt"].ap()[:, 0:2048]
    sb["lnf_l"] = sb["qwt"].ap()[:, 2048:4096]
    sb["gyf_h"] = sb["qwt"].ap()[:, 4096:6144]
    sb["gyf_l"] = sb["qwt"].ap()[:, 6144:8192]
    wpb = sb["wpool"].ap().bitcast(BF16)

    pb = [nc.alloc_psum_tensor(f"pb{i}", [128, 512], F32) for i in range(7)]
    pbq = nc.alloc_psum_tensor("pbq", [128, 1024], BF16)
    PBQ = "q"

    cst = []
    for name, dst in (("gq_t", "gq"), ("gka_t", "gka"), ("gsf_t", "gsf"),
                      ("tri", "tri"), ("ident", "ident"), ("ones1", "ones1"),
                      ("sel8", "sel8"), ("head_sel", "head_sel"),
                      ("ln_sb", "lnsb"), ("gn_sb", "gnsb"),
                      ("fbias", "bias2"), ("cmask", "cmask")):
        cst.append(b.emit("sync", lambda e, d=sb[dst], s=dr[name]:
                          e.dma_start(out=d[:, :], in_=s.ap()),
                          inc="cst", by=16))
    CST = cst[-1]
    t_bias = b.emit("vector", lambda e: e.tensor_scalar_mul(
        sb["bias2"][:, :], sb["bias2"][:, :], 0.5), deps=[CST], inc="ve")
    t_idb = b.emit("vector", lambda e: e.tensor_copy(
        sb["identb"][:, :], sb["ident"][:, :]), deps=[CST], inc="ve")
    t_eps = b.emit("vector", lambda e: e.memset(sb["epsc"][:, :], EPS),
                   inc="ve")

    wst = {"n": 0, "free": [None] * 4}

    def wload(src_ap):
        slot = wst["n"] % 4
        wst["n"] += 1
        dstt = sb["wpool"][:, slot * 1024:(slot + 1) * 1024]
        tok = b.emit("sync", lambda e, d=dstt, s=src_ap:
                     e.dma_start(out=d, in_=s),
                     deps=[wst["free"][slot]], inc=f"wld{slot}", by=16)
        return slot, dstt, tok

    def wfree(slot, tok):
        wst["free"][slot] = tok

    wsp_tok = {}

    def wload_sp2(key_a, key_b):
        ss = (wst["n"] // 2) % 2
        wst["n"] += 2
        dstt = wpb[:, ss * 4096:(ss + 1) * 4096]
        r0 = wsp_row[key_a]
        assert wsp_row[key_b] == r0 + 128, (key_a, key_b)
        src = dr["wsp"].ap()[r0:r0 + 256, :].rearrange(
            "(two p) c -> p two c", p=128)
        dst3 = dstt.rearrange("p (two c) -> p two c", two=2)
        dd = [wst["free"][2 * ss], wst["free"][2 * ss + 1],
              wsp_tok[key_a], wsp_tok[key_b]]
        tok = None
        for hv in range(2):
            tok = b.emit("sync", lambda e,
                         d=dst3[:, :, hv * 1024:(hv + 1) * 1024],
                         s=src[:, :, hv * 1024:(hv + 1) * 1024]:
                         e.dma_start(out=d, in_=s),
                         deps=dd, inc=f"wlp{ss}", by=16)
        return ss, dstt, tok

    def wfree2(ss, tok):
        wst["free"][2 * ss] = tok
        wst["free"][2 * ss + 1] = tok

    def emit_wsplit_gen():
        kstg = sb["kf"].ap().bitcast(BF16)
        last = {0: None, 1: None}
        for i, key in enumerate(wsp_keys):
            nm, k_, hf_ = key
            if hf_ is None:
                src = dr[nm][k_ * 128:(k_ + 1) * 128, :]
            else:
                src = dr[nm][k_ * 128:(k_ + 1) * 128,
                             hf_ * 1024:(hf_ + 1) * 1024]
            slot, wt, wtok = wload(src)
            par = i % 2
            stg = kstg[:, par * 2048:(par + 1) * 2048]
            hi = b.emit("scalar", lambda e, o=stg[:, 0:1024], i_=wt:
                        e.copy(o, i_), deps=[wtok, last[par]], inc="sc")
            lo = b.emit("vector", lambda e, o=stg[:, 1024:2048], i_=wt,
                        h_=stg[:, 0:1024]: e.tensor_sub(o, i_, h_),
                        deps=[hi], inc="ve")
            wfree(slot, lo)
            r0 = wsp_row[key]
            gw = f"gws{par}"
            stok = b.emit("gpsimd", lambda e, d=dr["wsp"][r0:r0 + 128, :],
                          s_=stg: e.dma_start(out=d, in_=s_),
                          deps=[lo, (gw, b.cnt.get(gw, 0))], inc=gw, by=16)
            wsp_tok[key] = stok
            last[par] = stok
            yield
        for h in range(H):
            b.note(("kfh", h), last[h // 4])

    def emit_qw_reload():
        toks = []
        for jj in range(2):
            r0 = QW_ROW + jj * 256
            src = dr["wsp"].ap()[r0:r0 + 256, :].rearrange(
                "(two p) c -> p two c", p=128)
            dst3 = sb["qwt"].ap()[:, jj * 4096:(jj + 1) * 4096].rearrange(
                "p (two c) -> p two c", two=2)
            toks.append(b.emit("sync", lambda e, d=dst3, s=src:
                        e.dma_start(out=d, in_=s),
                        deps=[("qst", b.cnt.get("qst", 0)),
                              ("te", b.cnt.get("te", 0)),
                              b.last("qwtv"), b.last("qwt")],
                        inc=f"qrl{jj}", by=16))
        b.note("qwt", toks)
        return toks

    def stats_pass(x_ap, mu_ap, r_ap, deps, skey):
        s0 = sb["sraw"][:, 0:1]
        s1 = sb["sraw"][:, 1:2]
        t0 = b.emit("vector", lambda e: e.reduce_sum(s0, x_ap, AX.X),
                    deps=deps + [b.last("s0")], inc="ve")
        t1 = b.emit("scalar", lambda e: e.activation(
            sb["fsc"][:, :], x_ap, AF.Square, accum_out=s1),
            deps=deps + [b.last("s1"), b.last("fsc")], inc="sc")
        b.note("fsc", t1)
        t2 = b.emit("vector", lambda e: e.tensor_scalar_mul(mu_ap, s0, 1.0 / D),
                    deps=[t0, b.last(("str", skey))], inc="ve")
        b.note("s0", t2)
        t3 = b.emit("vector", lambda e: e.tensor_scalar_mul(
            sb["small"][:, 0:1], s1, 1.0 / D),
            deps=[t1, b.last("sm0")], inc="ve")
        b.note("s1", t3)
        t4 = b.emit("vector", lambda e: e.tensor_tensor(
            sb["small"][:, 1:2], mu_ap, mu_ap, ALU.mult),
            deps=[t2, b.last("sm1")], inc="ve")
        t5 = b.emit("vector", lambda e: e.tensor_sub(
            sb["small"][:, 0:1], sb["small"][:, 0:1], sb["small"][:, 1:2]),
            deps=[t3, t4], inc="ve")
        b.note("sm1", t5)
        t6 = b.emit("scalar", lambda e: e.activation(
            sb["small"][:, 2:3], sb["small"][:, 0:1], AF.Sqrt,
            bias=sb["epsc"][:, 0:1]),
            deps=[t5, t_eps, b.last("sm2")], inc="sc")
        b.note("sm0", t6)
        t7 = b.emit("vector", lambda e: e.reciprocal(r_ap, sb["small"][:, 2:3]),
                    deps=[t6, b.last(("str", skey))], inc="ve")
        b.note("sm2", t7)
        b.note(("str", skey), None)
        b.note(("stw", skey), t7)
        return t7

    def ln_fm(src_sb, src_off, mus, dname, deps_in):
        dh, dl = sb[dname + "_h"], sb[dname + "_l"]
        ln_toks = []
        for st in range(NST):
            xa = src_sb[:, src_off + st * 1024:src_off + (st + 1) * 1024]
            (mu, rr, rdy, skey) = mus[st]
            tk = b.emit("vector", lambda e,
                        o=sb["lnt"][:, st * 1024:(st + 1) * 1024],
                        i=xa, m=mu, r=rr:
                        e.tensor_scalar(o, i, m, r, ALU.subtract, ALU.mult),
                        deps=deps_in + [rdy, b.last(("lnt", st))], inc="ve")
            b.note_add(("str", skey), tk)
            ln_toks.append(tk)
        out_toks = []
        for fc in range(FC):
            bk = fc % 2
            tt = None
            for st in range(NST):
                ia = sb["lnt"][:, st * 1024 + fc * 128:st * 1024 + (fc + 1) * 128]
                dd = [ln_toks[st], CST] + (b.bw(bk) if st == 0 else [])
                tt = b.emit("tensor", lambda e,
                            o=pb[bk][:, st * 128:(st + 1) * 128], i=ia:
                            e.transpose(o, i, sb["ident"][:, :]),
                            deps=dd, inc=("te" if st == NST - 1 else None))
            b.set_w(bk, tt)
            fscr = sb["lnf"][:, fc * CH:(fc + 1) * CH]
            sc_ = sb["lnsb"][:, fc:fc + 1]
            bc_ = sb["lnsb"][:, FC + fc:FC + fc + 1]
            t32 = b.emit("vector", lambda e, o=fscr, i=pb[bk][:, 0:CH],
                         s=sc_, bb=bc_:
                         e.tensor_scalar(o, i, s, bb, ALU.mult, ALU.add),
                         deps=[tt, b.last(("fm32", fc))], inc="ve")
            b.add_r(bk, t32)
            b.note(("fm32", fc), t32)
            qd = [b.last("qwt")] if dname == "lnf" else []
            thi = b.emit("scalar", lambda e, o=dh[:, fc * CH:(fc + 1) * CH],
                         i=fscr: e.copy(o, i),
                         deps=[t32, b.last(("fmh", dname, fc))] + qd,
                         inc="sc")
            tlo = b.emit("vector", lambda e, o=dl[:, fc * CH:(fc + 1) * CH],
                         i=fscr, h_=dh[:, fc * CH:(fc + 1) * CH]:
                         e.tensor_sub(o, i, h_),
                         deps=[thi, b.last(("fml", dname, fc))], inc="ve")
            b.note(("fmh", dname, fc), [thi, tlo])
            b.note(("fml", dname, fc), tlo)
            if dname == "lnf":
                b.note_add("qwtv", [thi, tlo])
            out_toks.append([thi, tlo])
        for st in range(NST):
            b.note(("lnt", st), [p for pr in out_toks for p in pr])
        return out_toks

    def emit_norm(src_sb, src_off, mus, deps_in):
        ln_toks = []
        for st in range(NST):
            xa = src_sb[:, src_off + st * 1024:src_off + (st + 1) * 1024]
            (mu, rr, rdy, skey) = mus[st]
            tk = b.emit("vector", lambda e,
                        o=sb["lnt"][:, st * 1024:(st + 1) * 1024],
                        i=xa, m=mu, r=rr:
                        e.tensor_scalar(o, i, m, r, ALU.subtract, ALU.mult),
                        deps=deps_in + [rdy, b.last(("lnt", st))], inc="ve")
            b.note_add(("str", skey), tk)
            ln_toks.append(tk)
        return ln_toks

    def ln_proj_fused(src_sb, src_off, mus, dname, deps_in, wkeys, mm_fc,
                      pre_ln=None):
        dh, dl = sb[dname + "_h"], sb[dname + "_l"]
        if pre_ln is not None:
            ln_toks = pre_ln
        else:
            ln_toks = emit_norm(src_sb, src_off, mus, deps_in)

        def transp(fc):
            bk = 4 + (fc % 2)
            tt = None
            for st in range(NST):
                ia = sb["lnt"][:, st * 1024 + fc * 128:
                               st * 1024 + (fc + 1) * 128]
                dd = [ln_toks[st], CST] + (b.bw(bk) if st == 0 else [])
                tt = b.emit("tensor", lambda e,
                            o=pb[bk][:, st * 128:(st + 1) * 128], i=ia:
                            e.transpose(o, i, sb["ident"][:, :]),
                            deps=dd, inc=("te" if st == NST - 1 else None))
            b.set_w(bk, tt)
            return tt

        out_toks = []
        tt_fc = transp(0)
        wctx = {}
        for fc in range(FC):
            bk = 4 + (fc % 2)
            fscr = sb["lnf"][:, fc * CH:(fc + 1) * CH]
            sc_ = sb["lnsb"][:, fc:fc + 1]
            bc_ = sb["lnsb"][:, FC + fc:FC + fc + 1]
            t32 = b.emit("vector", lambda e, o=fscr, i=pb[bk][:, 0:CH],
                         s=sc_, bb=bc_:
                         e.tensor_scalar(o, i, s, bb, ALU.mult, ALU.add),
                         deps=[tt_fc, b.last(("fm32", fc))], inc="ve")
            b.add_r(bk, t32)
            b.note(("fm32", fc), t32)
            qd = [b.last("qwt")] if dname == "lnf" else []
            thi = b.emit("scalar", lambda e, o=dh[:, fc * CH:(fc + 1) * CH],
                         i=fscr: e.copy(o, i),
                         deps=[t32, b.last(("fmh", dname, fc))] + qd,
                         inc="sc")
            tlo = b.emit("vector", lambda e, o=dl[:, fc * CH:(fc + 1) * CH],
                         i=fscr, h_=dh[:, fc * CH:(fc + 1) * CH]:
                         e.tensor_sub(o, i, h_),
                         deps=[thi, b.last(("fml", dname, fc))], inc="ve")
            b.note(("fmh", dname, fc), [thi, tlo])
            b.note(("fml", dname, fc), tlo)
            if dname == "lnf":
                b.note_add("qwtv", [thi, tlo])
            out_toks.append([thi, tlo])
            if fc % 2 == 0:
                wctx["cur"] = wload_sp2(*wkeys[fc // 2])
            if fc + 1 < FC:
                tt_fc = transp(fc + 1)
            ss_, wt2, wtok = wctx["cur"]
            wt = wt2[:, (fc % 2) * 2048:(fc % 2 + 1) * 2048]
            lastmm = mm_fc(fc, tlo, wt, wtok)
            if fc % 2 == 1:
                wfree2(ss_, lastmm)
        for st in range(NST):
            b.note(("lnt", st), [p for pr in out_toks for p in pr])
        return out_toks

    def mm_tm(dh, dl, vmm):
        def mm_fc(fc, tlo, wt, wtok):
            for p, (xs, ws) in enumerate(((0, 0), (0, 1), (1, 0))):
                lsrc = dh if xs == 0 else dl
                for j in range(NST * 2):
                    st, half = divmod(j, 2)
                    dd = [wtok, tlo]
                    if fc == 0 and p == 0:
                        dd += b.bw(j)
                    vmm[j] = b.emit(
                        "tensor", lambda e, p_=pb[j],
                        l=lsrc[:, fc * CH + st * 128:fc * CH + (st + 1) * 128],
                        r=wt[:, ws * 1024 + half * 512:
                             ws * 1024 + (half + 1) * 512],
                        s=(fc == 0 and p == 0),
                        sp=(fc == FC - 1 and p == 2):
                        e.matmul(p_[:, 0:512], l, r, start=s, stop=sp),
                        deps=dd,
                        inc=("te" if ((fc == FC - 1 and p == 2)
                                      or (p == 2 and j == NST * 2 - 1))
                             else None))
            return vmm[-1]
        return mm_fc

    def proj_evac(out_sb, vmm, okey, extra=()):
        proj_toks = []
        for j in range(NST * 2):
            b.set_w(j, vmm[j])
            da = out_sb[:, j * 512:(j + 1) * 512]
            tk = b.emit("scalar", lambda e, o=da, i=pb[j][:, 0:512]:
                        e.copy(o, i),
                        deps=[vmm[j], b.last((okey, j))] + list(extra),
                        inc="sc")
            b.add_r(j, tk)
            b.note((okey, j), tk)
            proj_toks.append(tk)
        return proj_toks

    def proj_fm(wname, in_sb, in_tok, out_sb, silu=False):
        mm_bank = [None] * 4
        for kk in range(0, FC, 2):
          ss_, wt2, wtok = wload_sp2((wname, kk, None), (wname, kk + 1, None))
          for dk in range(2):
            k = kk + dk
            wt = wt2[:, dk * 2048:(dk + 1) * 2048]
            last_of_k = None
            if True:
              for p, (xs, ws) in enumerate(((0, 0), (0, 1), (1, 0))):
                rsrc = sb["lnf_h"] if xs == 0 else sb["lnf_l"]
                for h in range(H):
                    bk = h // 2
                    dd = [wtok, in_tok]
                    if k == 0 and p == 0 and h % 2 == 0:
                        dd += b.bw(bk)
                    want_tok = (k == FC - 1 and p == 2) and (h % 2 == 1)
                    want_slot = (p == 2 and h == H - 1)
                    mm = b.emit("tensor", lambda e,
                                p_=pb[bk][:, (h % 2) * CH:(h % 2 + 1) * CH],
                                l=wt[:, ws * 1024 + h * 128:
                                     ws * 1024 + (h + 1) * 128],
                                r=rsrc[:, k * CH:(k + 1) * CH],
                                s=(k == 0 and p == 0 and h % 2 == 0),
                                sp=(k == FC - 1 and p == 2 and h % 2 == 1):
                                e.matmul(p_, l, r, start=s, stop=sp),
                                deps=dd,
                                inc=("te" if (want_tok or want_slot)
                                     else None))
                    if want_tok:
                        mm_bank[bk] = mm
                    if want_slot:
                        last_of_k = mm
          wfree2(ss_, last_of_k)
        out_toks = []
        for bk in range(4):
            b.set_w(bk, mm_bank[bk])
            da = out_sb[:, bk * 2 * CH:(bk + 1) * 2 * CH]
            if silu:
                tk = b.emit("scalar", lambda e, o=da, i=pb[bk][:, 0:512]:
                            e.activation(o, i, AF.Silu),
                            deps=[mm_bank[bk], b.last(("pf", id(out_sb), bk))],
                            inc="sc")
            else:
                tk = b.emit("scalar", lambda e, o=da, i=pb[bk][:, 0:512]:
                            e.copy(o, i),
                            deps=[mm_bank[bk], b.last(("pf", id(out_sb), bk))],
                            inc="sc")
            b.add_r(bk, tk)
            b.note(("pf", id(out_sb), bk), tk)
            out_toks.append(tk)
        return out_toks

    def proj_tm(wname, out_sb, in_tok, okey, extra=()):
        vmm = [None] * (NST * 2)
        for kk in range(0, FC, 2):
          ss_, wt2, wtok = wload_sp2((wname, kk, None), (wname, kk + 1, None))
          for dk in range(2):
            k = kk + dk
            wt = wt2[:, dk * 2048:(dk + 1) * 2048]
            if True:
              for p, (xs, ws) in enumerate(((0, 0), (0, 1), (1, 0))):
                lsrc = sb["lnf_h"] if xs == 0 else sb["lnf_l"]
                for j in range(NST * 2):
                    st, half = divmod(j, 2)
                    bk = j
                    dd = [wtok, in_tok]
                    if k == 0 and p == 0:
                        dd += b.bw(bk)
                    vmm[j] = b.emit(
                        "tensor", lambda e, p_=pb[bk],
                        l=lsrc[:, k * CH + st * 128:k * CH + (st + 1) * 128],
                        r=wt[:, ws * 1024 + half * 512:
                             ws * 1024 + (half + 1) * 512],
                        s=(k == 0 and p == 0),
                        sp=(k == FC - 1 and p == 2):
                        e.matmul(p_[:, 0:512], l, r, start=s, stop=sp),
                        deps=dd,
                        inc=("te" if ((k == FC - 1 and p == 2)
                                      or (p == 2 and j == NST * 2 - 1))
                             else None))
          wfree2(ss_, vmm[-1])
        out_toks = []
        for j in range(NST * 2):
            bk = j
            b.set_w(bk, vmm[j])
            da = out_sb[:, j * 512:(j + 1) * 512]
            tk = b.emit("scalar", lambda e, o=da, i=pb[bk][:, 0:512]:
                        e.copy(o, i),
                        deps=[vmm[j], b.last((okey, j))] + list(extra),
                        inc="sc")
            b.add_r(bk, tk)
            b.note((okey, j), tk)
            out_toks.append(tk)
        return out_toks

    def rope_tm(buf, rpar, rtoks, in_toks, sign):
        scr = sb["ropescr"]
        done = []
        for st in range(NST):
            bufst = buf[:, st * 1024:(st + 1) * 1024]
            x1 = bufst.rearrange("p (h b) -> p h b", h=H)[:, :, 0:64]
            x2 = bufst.rearrange("p (h b) -> p h b", h=H)[:, :, 64:128]
            co = sb["rope2"][:, rpar * 256 + st * 64:
                             rpar * 256 + (st + 1) * 64]
            si = sb["rope2"][:, rpar * 256 + 128 + st * 64:
                             rpar * 256 + 128 + (st + 1) * 64]
            cb = co[:, None, :].broadcast_to([128, H, 64])
            sib = si[:, None, :].broadcast_to([128, H, 64])
            s1 = scr[:, 0:512].rearrange("p (h b) -> p h b", h=H)
            s2 = scr[:, 512:1024].rearrange("p (h b) -> p h b", h=H)
            s3 = scr[:, 1024:1536].rearrange("p (h b) -> p h b", h=H)
            s4 = scr[:, 1536:2048].rearrange("p (h b) -> p h b", h=H)
            dd = [in_toks, rtoks, b.last("ropescr")] + \
                [b.last(("hf", i)) for i in range(8)]
            t1 = b.emit("vector", lambda e, o=s1, a=x1, bb=cb:
                        e.tensor_tensor(o, a, bb, ALU.mult), deps=dd, inc="ve")
            t2 = b.emit("vector", lambda e, o=s2, a=x2, bb=sib:
                        e.tensor_tensor(o, a, bb, ALU.mult), deps=dd, inc="ve")
            t3 = b.emit("vector", lambda e, o=s3, a=x1, bb=sib:
                        e.tensor_tensor(o, a, bb, ALU.mult), deps=dd, inc="ve")
            t4 = b.emit("vector", lambda e, o=s4, a=x2, bb=cb:
                        e.tensor_tensor(o, a, bb, ALU.mult), deps=dd, inc="ve")
            op1 = ALU.subtract if sign > 0 else ALU.add
            t5 = b.emit("vector", lambda e, o=x1, a=s1, bb=s2, op=op1:
                        e.tensor_tensor(o, a, bb, op),
                        deps=[t1, t2, t3, t4], inc="ve")
            if sign > 0:
                t6 = b.emit("vector", lambda e, o=x2, a=s3, bb=s4:
                            e.tensor_tensor(o, a, bb, ALU.add),
                            deps=[t2, t3, t4], inc="ve")
            else:
                t6 = b.emit("vector", lambda e, o=x2, a=s4, bb=s3:
                            e.tensor_tensor(o, a, bb, ALU.subtract),
                            deps=[t2, t3, t4], inc="ve")
            b.note("ropescr", [t5, t6])
            for i in range(8):
                b.note_add(("hf", i), [t5, t6])
            done.append([t5, t6])
        return done

    def emit_qw_prep():
        tz = b.emit("vector", lambda e: e.memset(sb["maxc"][:, :], 0.0),
                    deps=[CST], inc="ve")
        prev = tz
        for o in range(FC):
            slot, wt, wtok = wload(dr["fw"][o * 128:(o + 1) * 128, :])
            t1 = b.emit("vector", lambda e, i=wt: e.reduce_sum(
                sb["small"][:, 3:4], i, AX.X, apply_absolute_value=True),
                deps=[wtok, b.last("sm3")], inc="ve")
            prev = b.emit("vector", lambda e: e.tensor_add(
                sb["maxc"][:, :], sb["maxc"][:, :], sb["small"][:, 3:4]),
                deps=[t1, prev], inc="ve")
            b.note("sm3", prev)
            wfree(slot, t1)
        dd = b.bw(6)
        tT = b.emit("tensor", lambda e: e.transpose(
            pb[6][0:1, 0:128], sb["maxc"][:, :], sb["ident"][:, :]),
            deps=[prev, CST] + dd, inc="te")
        b.set_w(6, tT)
        b.note("maxc", tT)
        tR = b.emit("vector", lambda e: e.reduce_sum(
            sb["srow"][0:1, 0:1], pb[6][0:1, 0:128], AX.X),
            deps=[tT, b.last("srow")], inc="ve")
        b.add_r(6, tR)
        tS = b.emit("vector", lambda e: e.tensor_scalar_mul(
            sb["srow"][0:1, 0:1], sb["srow"][0:1, 0:1], 1.0 / (D * D)),
            deps=[tR], inc="ve")
        dd = b.bw(6)
        tB = b.emit("tensor", lambda e: e.matmul(
            pb[6][:, 0:1], sb["ones1"][0:1, :], sb["srow"][0:1, 0:1],
            start=True, stop=True), deps=[tS, CST] + dd, inc="te")
        b.set_w(6, tB)
        b.note("srow", tB)
        tW = b.emit("vector", lambda e: e.tensor_copy(
            sb["wsc"][:, :], pb[6][:, 0:1]), deps=[tB], inc="ve")
        b.add_r(6, tW)
        tTh = b.emit("vector", lambda e: e.tensor_scalar_mul(
            sb["thc"][:, 0:1], sb["wsc"][:, :], 0.5), deps=[tW], inc="ve")
        tTh2 = b.emit("vector", lambda e: e.tensor_scalar_mul(
            sb["thc"][:, 1:2], sb["thc"][:, 0:1], -1.0), deps=[tTh], inc="ve")
        qv = None
        for o in range(FC):
            slot, wt, wtok = wload(dr["fw"][o * 128:(o + 1) * 128, :])
            tg = b.emit("vector", lambda e, i=wt: e.tensor_scalar(
                sb["fsc"][:, :], i, sb["thc"][:, 0:1], None, ALU.is_gt),
                deps=[wtok, tTh, b.last("fsc")], inc="ve")
            tl = b.emit("vector", lambda e, i=wt: e.tensor_scalar(
                sb["lnt"][:, 0:1024], i, sb["thc"][:, 1:2], None, ALU.is_lt),
                deps=[wtok, tTh2, b.last(("lnt", 0))], inc="ve")
            tq = b.emit("vector", lambda e: e.tensor_sub(
                sb["qxf"][:, 0:1024], sb["fsc"][:, 0:1024],
                sb["lnt"][:, 0:1024]),
                deps=[tg, tl, b.last(("qxf", 0))], inc="ve")
            b.note(("lnt", 0), tq)
            b.note("fsc", tq)
            wfree(slot, tq)
            tt = None
            for kc in range(FC):
                dd = [tq, t_idb] + (b.bw(PBQ) if kc == 0 else [])
                tt = b.emit("tensor", lambda e,
                            o_=pbq[:, kc * 128:(kc + 1) * 128],
                            i=sb["qxf"][:, kc * 128:(kc + 1) * 128]:
                            e.transpose(o_, i, sb["identb"][:, :]),
                            deps=dd, inc=("te" if kc == FC - 1 else None))
            b.set_w(PBQ, tt)
            b.note(("qxf", 0), tt)
            qv = b.emit("vector", lambda e, o_=o: e.tensor_copy(
                sb["qwt"].ap().rearrange("p (k o) -> p k o", k=FC)
                [:, :, o_ * 128:(o_ + 1) * 128],
                pbq[:, :].rearrange("p (k o) -> p k o", o=128)),
                deps=[tt, b.last("qwtv"), b.last("qwt")], inc="ve")
            b.add_r(PBQ, qv)
            b.note_add("qwt", qv)
        for sub in range(4):
            r0 = QW_ROW + sub * 128
            b.emit("sync", lambda e, d=dr["wsp"][r0:r0 + 128, :],
                   s_=sb["qwt"].ap()[:, sub * 2048:(sub + 1) * 2048]:
                   e.dma_start(out=d, in_=s_),
                   deps=[qv], inc="qst", by=16)
        return qv, tTh

    def emit_ffn_outer(src_dram, dst_dram, qw_tok, ws_tok, rd_marks,
                       war_marks, bg=None, qw_loader=None):
        def pump(n):
            if bg is not None:
                for _ in range(n):
                    if next(bg, "END") == "END":
                        return

        tz = b.emit("vector", lambda e: e.memset(sb["maxc"][:, :], 0.0),
                    deps=[b.last("maxc")], inc="ve")
        prevmax = tz
        for tt in range(NT):
            par = tt % 4
            xa = sb["xchunk"][:, par * 1024:(par + 1) * 1024]
            xtok = b.emit("sync", lambda e, d=xa,
                          s=dr[src_dram][tt * 128:(tt + 1) * 128, :]:
                          e.dma_start(out=d, in_=s),
                          deps=[b.last(("xc", par))] + rd_marks(tt),
                          inc=f"ldx{par}", by=16)
            mu = sb["stats"][:, tt:tt + 1]
            rr = sb["stats"][:, NT + tt:NT + tt + 1]
            ts_ = stats_pass(xa, mu, rr, [xtok], tt)
            tln = b.emit("vector", lambda e, i=xa, m=mu, r=rr: e.tensor_scalar(
                sb["lnt"][:, 0:1024], i, m, r, ALU.subtract, ALU.mult),
                deps=[ts_, xtok, b.last(("lnt", 0))], inc="ve")
            b.note(("xc", par), tln)
            b.note_add(("str", tt), tln)
            tmx = b.emit("vector", lambda e: e.reduce_max(
                sb["small"][:, 4:5], sb["lnt"][:, 0:1024], AX.X,
                apply_absolute_value=True),
                deps=[tln, b.last("sm4")], inc="ve")
            b.note(("lnt", 0), tmx)
            prevmax = b.emit("vector", lambda e: e.tensor_max(
                sb["maxc"][:, :], sb["maxc"][:, :], sb["small"][:, 4:5]),
                deps=[tmx, prevmax], inc="ve")
            b.note("sm4", prevmax)
            pump(9)
        if qw_loader is not None:
            qw_tok = qw_loader()
        dd = b.bw(6)
        tT = b.emit("tensor", lambda e: e.transpose(
            pb[6][0:1, 0:128], sb["maxc"][:, :], sb["ident"][:, :]),
            deps=[prevmax, CST] + dd, inc="te")
        b.set_w(6, tT)
        b.note("maxc", tT)
        tR = b.emit("vector", lambda e: e.reduce_max(
            sb["srow"][0:1, 0:1], pb[6][0:1, 0:128], AX.X),
            deps=[tT, b.last("srow")], inc="ve")
        b.add_r(6, tR)
        dd = b.bw(6)
        tB = b.emit("tensor", lambda e: e.matmul(
            pb[6][:, 0:4], sb["ones1"][0:1, :],
            sb["srow"][0:1, 0:1].broadcast_to([1, 4]), start=True, stop=True),
            deps=[tR, CST] + dd, inc="te")
        b.set_w(6, tB)
        b.note("srow", tB)
        tC = b.emit("vector", lambda e: e.tensor_copy(
            sb["gmaxc"][:, :], pb[6][:, 0:4]),
            deps=[tB, b.last("gmaxc")], inc="ve")
        b.add_r(6, tC)
        if collective:
            tSd = b.emit("gpsimd", lambda e: e.dma_start(
                out=dr["bmax_i"][:, :], in_=sb["gmaxc"][:, :]),
                deps=[tC, b.last("gbc")], inc="gbc", by=16)
            tCC = b.emit("gpsimd", lambda e: e.collective_compute(
                "AllReduce", ALU.max, replica_groups=[list(range(n_cores))],
                ins=[dr["bmax_i"].ap().opt()], outs=[dr["bmax_o"].ap().opt()]),
                deps=[tSd], inc="cc")
            tL = b.emit("gpsimd", lambda e: e.dma_start(
                out=sb["gmaxc"][:, :], in_=dr["bmax_o"][:, :]),
                deps=[tCC], inc="gbc", by=16)
            b.note("gbc", tL)
        else:
            tL = tC
        b.note("gmaxc", tL)
        t1 = b.emit("vector", lambda e: e.reciprocal(
            sb["qsc"][:, :], sb["gmaxc"][:, 0:1]), deps=[tL], inc="ve")
        qsc_tok = b.emit("vector", lambda e: e.tensor_scalar_mul(
            sb["qsc"][:, :], sb["qsc"][:, :], 127.0), deps=[t1], inc="ve")
        t3 = b.emit("vector", lambda e: e.tensor_tensor(
            sb["sic"][:, :], sb["gmaxc"][:, 0:1], sb["wsc"][:, :], ALU.mult),
            deps=[tL, ws_tok], inc="ve")
        sic_tok = b.emit("vector", lambda e: e.tensor_scalar_mul(
            sb["sic"][:, :], sb["sic"][:, :], 0.5 / 127.0), deps=[t3],
            inc="ve")
        out_marks = {}
        gfbb = sb["gfb"].ap().bitcast(BF16)

        def q_stage(tt):
            par = tt % 2
            par4 = tt % 4
            xa = sb["xchunk"][:, par4 * 1024:(par4 + 1) * 1024]
            xtok = b.emit("sync", lambda e, d=xa,
                          s=dr[src_dram][tt * 128:(tt + 1) * 128, :]:
                          e.dma_start(out=d, in_=s),
                          deps=[b.last(("xc", par4))] + rd_marks(tt),
                          inc=f"ldx{par4}", by=16)
            mu = sb["stats"][:, tt:tt + 1]
            rr = sb["stats"][:, NT + tt:NT + tt + 1]
            sm = sb["small"][:, 5 + par:6 + par]
            trq = b.emit("vector", lambda e, r=rr, o=sm: e.tensor_tensor(
                o, r, sb["qsc"][:, :], ALU.mult),
                deps=[qsc_tok, b.last(("sm5", par)), b.last(("stw", tt))],
                inc="ve")
            lnq = sb["lnt"][:, par * 1024:(par + 1) * 1024]
            tqa = b.emit("vector", lambda e, i=xa, m=mu, o=lnq, s_=sm:
                         e.tensor_scalar(o, i, m, s_, ALU.subtract, ALU.mult),
                         deps=[xtok, trq, b.last(("lnt", par))], inc="ve")
            b.note(("sm5", par), tqa)
            b.note_add(("str", tt), tqa)
            sgn = gfbb[:, par * 1024:(par + 1) * 1024]
            gkey = ("pf", id(sb["gfb"]), par)
            tsg = b.emit("scalar", lambda e, o=sgn, i=lnq: e.activation(
                o, i, AF.Sign), deps=[tqa, b.last(gkey)], inc="sc")
            ab = sb["vf"][:, par * 1024:(par + 1) * 1024]
            vkeys = [("vf", 2 * par), ("vf", 2 * par + 1)]
            tabs = b.emit("scalar", lambda e, o=ab, i=lnq: e.activation(
                o, i, AF.Abs),
                deps=[tqa, b.last(vkeys[0]), b.last(vkeys[1])], inc="sc")
            trc = b.emit("scalar", lambda e, o=sb["qbf"][:, :], i=ab:
                         e.activation(o, i, AF.Copy, bias=128.0),
                         deps=[tabs, b.last("qbf")], inc="sc")
            b.note(vkeys[0], trc)
            b.note(vkeys[1], trc)
            qx = sb["qxf"][:, par * 1024:(par + 1) * 1024]
            tr3 = b.emit("vector", lambda e, o=qx, s_=sgn:
                         e.scalar_tensor_tensor(o, sb["qbf"][:, :], 128.0,
                                                s_, ALU.subtract, ALU.mult),
                         deps=[trc, tsg, b.last(("qxf", par))], inc="ve")
            b.note("qbf", tr3)
            b.note(gkey, tr3)
            b.note(("lnt", par), [tsg, tabs])
            tt_ = None
            for kc in range(FC):
                dd = [tr3, t_idb] + (b.bw(PBQ) if kc == 0 else [])
                tt_ = b.emit("tensor", lambda e,
                             o_=pbq[:, kc * 128:(kc + 1) * 128],
                             i=qx[:, kc * 128:(kc + 1) * 128]:
                             e.transpose(o_, i, sb["identb"][:, :]),
                             deps=dd, inc=("te" if kc == FC - 1 else None))
            b.set_w(PBQ, tt_)
            tqf = b.emit("scalar", lambda e, o=qx: e.copy(
                o, pbq[:, :]), deps=[tt_], inc="sc")
            b.add_r(PBQ, tqf)
            return (tt, xa, qx, tqf, xtok)

        def m_stage(ctxq):
            tt, xa, qx, tqf, xtok = ctxq
            par = tt % 2
            par4 = tt % 4
            mm_last = [None, None]
            for kc in range(FC):
                for half in range(2):
                    bk = half
                    dd = [tqf, qw_tok]
                    if kc == 0:
                        dd += b.bw(bk)
                    mm_last[half] = b.emit(
                        "tensor", lambda e, p_=pb[bk],
                        l=qx[:, kc * 128:(kc + 1) * 128],
                        r=sb["qwt"][:, kc * 1024 + half * 512:
                                    kc * 1024 + (half + 1) * 512],
                        s=(kc == 0), sp=(kc == FC - 1):
                        e.matmul(p_[:, 0:512], l, r, start=s, stop=sp),
                        deps=dd, inc=("te" if kc == FC - 1 else None))
            b.note(("qxf", par), [mm_last[0], mm_last[1]])
            ev = []
            for half in range(2):
                bk = half
                b.set_w(bk, mm_last[half])
                o1 = sb["fsc"][:, half * 512:(half + 1) * 512]
                e1 = b.emit("vector", lambda e, o=o1, i=pb[bk][:, 0:512],
                            x_=xa[:, half * 512:(half + 1) * 512]:
                            e.scalar_tensor_tensor(o, i, sb["sic"][:, 0:1],
                                                   x_, ALU.mult, ALU.add),
                            deps=[mm_last[half], sic_tok, xtok,
                                  b.last("fsc")], inc="ve")
                b.add_r(bk, e1)
                e2 = b.emit("vector", lambda e, o=o1,
                            bb=sb["bias2"][:, half * 512:(half + 1) * 512]:
                            e.tensor_add(o, o, bb), deps=[e1, t_bias],
                            inc="ve")
                ev.append(e2)
            b.note(("xc", par4), ev)
            gs_name = f"gst{tt % 4}"
            drain = [(gs_name, b.cnt.get(gs_name, 0))]
            stok = b.emit("gpsimd", lambda e,
                          d=dr[dst_dram][tt * 128:(tt + 1) * 128, :],
                          s=sb["fsc"][:, 0:1024]: e.dma_start(out=d, in_=s),
                          deps=ev + war_marks(tt) + drain, inc=gs_name, by=16)
            b.note("fsc", stok)
            out_marks[tt] = stok

        prevq = None
        for tt in range(NT):
            cur = q_stage(tt)
            if prevq is not None:
                m_stage(prevq)
            prevq = cur
        m_stage(prevq)
        return out_marks

    def emit_layer(li, xin, xout, rd_marks, war_marks, mode="merged",
                   ctx=None):
        out_marks = {}
        if mode != "B":
            tz = b.emit("vector", lambda e: e.memset(sb["sst"][:, :], 0.0),
                        deps=[b.last("sst")], inc="ve")
            b.note("sst", tz)
        if ctx is None:
            ctx = {}
        ctx.setdefault("mus", {})
        ctx.setdefault("kv", {})
        snap = sb["ktm"].ap()[:, 0:1024]
        sin_m = sb["ktm"].ap()[:, 1024:2048]
        xld = {}
        rld = {}

        def load_chunk(c):
            par = c % 2
            toks = []
            for st in range(NST):
                t0 = c * CH + st * 128
                slotid = par * NST + st
                da = sb["xchunk"][:, slotid * 1024:(slotid + 1) * 1024]
                toks.append(b.emit(
                    "sync", lambda e, d=da, s=dr[xin][t0:t0 + 128, :]:
                    e.dma_start(out=d, in_=s),
                    deps=[b.last(("xc", slotid))] + rd_marks(c * NST + st),
                    inc=f"ldx{slotid}", by=16))
            xld[c] = toks
            if mode != "B":
                mus = []
                for st in range(NST):
                    slotid = par * NST + st
                    xa = sb["xchunk"][:, slotid * 1024:(slotid + 1) * 1024]
                    mu = sb["stats"][:, c * NST + st:c * NST + st + 1]
                    rr = sb["stats"][:, NT + c * NST + st:
                                     NT + c * NST + st + 1]
                    rdy = stats_pass(xa, mu, rr, [toks[st]], c * NST + st)
                    mus.append((mu, rr, rdy, c * NST + st))
                ctx["mus"][c] = mus
            rts = []
            for t, k in ((0, "rope_c"), (1, "rope_s")):
                for st in range(NST):
                    t0 = c * CH + st * 128
                    da = sb["rope2"][:, par * 256 + t * 128 + st * 64:
                                     par * 256 + t * 128 + (st + 1) * 64]
                    rts.append(b.emit(
                        "scalar", lambda e, d=da, s=dr[k][t0:t0 + 128, :]:
                        e.dma_start(out=d, in_=s),
                        deps=[b.last(("rp", par))], inc=f"ldr{par}_{t}",
                        by=16))
            rld[c] = rts

        load_chunk(0)
        if NCH > 1:
            load_chunk(1)
        for c in range(NCH):
            par = c % 2
            mus = ctx["mus"][c]
            if mode == "B":
                kstore, vstore, snstore = ctx["kv"][c]
                kld = b.emit("scalar", lambda e, s_=dr["kvk"][
                    c * 128:(c + 1) * 128, :]:
                    e.dma_start(out=sb["kf"][:, :], in_=s_),
                    deps=[kstore] + [b.last(("kfh", hh)) for hh in range(H)],
                    inc="ldkf", by=16)
                kf_toks = [kld] * H
                for hh in range(H):
                    b.note(("kfh", hh), kld)
                vld = b.emit("scalar", lambda e, s_=dr["kvv"][
                    c * 128:(c + 1) * 128, :]:
                    e.dma_start(out=sb["vf"][:, :], in_=s_),
                    deps=[vstore] + [b.last(("vf", j)) for j in range(4)],
                    inc="ldvf", by=16)
                v_toks = [vld] * (NST * 2)
                for j in range(NST * 2):
                    b.note(("vf", j), vld)
                snld = b.emit("scalar", lambda e, s_=dr["snapb"][
                    c * 128:(c + 1) * 128, :]:
                    e.dma_start(out=snap, in_=s_),
                    deps=[snstore, b.last(("ktmj", 0)), b.last(("ktmj", 1)),
                          ("ldsn", b.cnt.get("ldsn", 0))],
                    inc="ldsn", by=16)
                scur_toks = []
                for h in range(H):
                    tk = b.emit("vector", lambda e,
                                o=sb["sst"][:, h * 128:(h + 1) * 128],
                                i=sin_m[:, h * 128:(h + 1) * 128],
                                gc=_gp(h, CH * c),
                                sn=snap[:, h * 128:(h + 1) * 128]:
                                e.scalar_tensor_tensor(o, i, gc, sn, ALU.mult,
                                                       ALU.add),
                                deps=[snld, ctx["sin_tok"],
                                      b.last("sst")], inc="ve")
                    scur_toks.append(tk)
                    b.note_add(("ktmj", 0), tk)
                    b.note_add(("ktmj", 1), tk)
                    b.note_add(("ktmj", 2), tk)
                    b.note_add(("ktmj", 3), tk)
                b.note("sst", scur_toks)
                kro = None
                vg_toks = None
                ka_toks = None
                vmm = [None] * (NST * 2)
                lnf_toks = ln_proj_fused(
                    sb["xchunk"], par * NST * 1024, mus, "lnf", [xld[c]],
                    [((f"wq{li}", kk, None), (f"wq{li}", kk + 1, None))
                     for kk in range(0, FC, 2)],
                    mm_tm(sb["lnf_h"], sb["lnf_l"], vmm),
                    pre_ln=ctx.get("pre_ln", {}).pop(c, None))
                q_toks = proj_evac(sb["qtm"], vmm, "qtmj",
                                   extra=[b.last(("ysb", hh))
                                          for hh in range(H)])
                lnf_all = b.emit("tensor", lambda e: e.nop(), deps=lnf_toks,
                                 inc="tnp")
            if mode != "B":
                vmm = [None] * (NST * 2)
                lnf_toks = ln_proj_fused(
                    sb["xchunk"], par * NST * 1024, mus, "lnf", [xld[c]],
                    [((f"wk{li}", kk, None), (f"wk{li}", kk + 1, None))
                     for kk in range(0, FC, 2)],
                    mm_tm(sb["lnf_h"], sb["lnf_l"], vmm),
                    pre_ln=ctx.get("pre_ln", {}).pop(c, None))
                k_toks = proj_evac(sb["ktm"], vmm, "ktmj")
                lnf_all = b.emit("tensor", lambda e: e.nop(), deps=lnf_toks,
                                 inc="tnp")
                v_toks = proj_tm(f"wv{li}", sb["vf"], lnf_all, "vf")
                kro = rope_tm(sb["ktm"], par, rld[c], k_toks, sign=-1)
                ka_toks = []
                for st in range(NST):
                    tk = b.emit("vector", lambda e,
                                o=sb["lnt"][:, st * 1024:(st + 1) * 1024],
                                i=sb["ktm"][:, st * 1024:(st + 1) * 1024],
                                g=sb["gka"][:, st * 1024:(st + 1) * 1024]:
                                e.tensor_mul(o, i, g),
                                deps=[CST, kro[st], b.last(("lnt", st))],
                                inc="ve")
                    ka_toks.append(tk)
                kf_toks = []
                for h in range(H):
                    bk = 4 + (h % 2)
                    tt = None
                    for st in range(NST):
                        ia = sb["lnt"][:, st * 1024 + h * 128:
                                       st * 1024 + (h + 1) * 128]
                        dd = [ka_toks[st], CST] + (b.bw(bk) if st == 0 else [])
                        tt = b.emit("tensor", lambda e,
                                    o=pb[bk][:, st * 128:(st + 1) * 128], i=ia:
                                    e.transpose(o, i, sb["ident"][:, :]),
                                    deps=dd, inc=("te" if st == NST - 1 else None))
                    b.set_w(bk, tt)
                    tk = b.emit("vector", lambda e,
                                o=sb["kf"][:, h * CH:(h + 1) * CH],
                                i=pb[bk][:, 0:NST * 128]: e.tensor_copy(o, i),
                                deps=[tt, b.last(("kfh", h))], inc="ve")
                    b.add_r(bk, tk)
                    b.note(("kfh", h), tk)
                    kf_toks.append(tk)
                for st in range(NST):
                    b.note(("lnt", st), kf_toks[-1])
                vg_toks = []
                for st in range(NST):
                    tk = b.emit("vector", lambda e,
                                o=sb["lnt"][:, st * 1024:(st + 1) * 1024],
                                i=sb["vf"][:, st * 1024:(st + 1) * 1024],
                                g=sb["gsf"][:, st * 1024:(st + 1) * 1024]:
                                e.tensor_mul(o, i, g),
                                deps=[CST, v_toks[2 * st], v_toks[2 * st + 1],
                                      b.last(("lnt", st))], inc="ve")
                    vg_toks.append(tk)
            if mode == "A":
                gsn = f"gsn{c % 2}"
                sn = b.emit("gpsimd", lambda e,
                            d=dr["snapb"][c * 128:(c + 1) * 128, :]:
                            e.dma_start(out=d, in_=sb["sst"][:, :]),
                            deps=[b.last("sst"),
                                  (gsn, b.cnt.get(gsn, 0))],
                            inc=gsn, by=16)
                ds_last = {}
                for grp in range(2):
                    bk = 2 + grp
                    first = True
                    for h in range(grp * 4, grp * 4 + 4):
                        for st in range(NST):
                            dd = [kro[st], vg_toks[st]]
                            if first:
                                dd += b.bw(bk)
                            ds_last[grp] = b.emit(
                                "tensor", lambda e, p_=pb[bk],
                                l=sb["ktm"][:, st * 1024 + h * 128:
                                            st * 1024 + (h + 1) * 128],
                                r=sb["lnt"][:, st * 1024 + h * 128:
                                            st * 1024 + (h + 1) * 128],
                                s=first, col=(h % 4) * 128,
                                sp=(h == grp * 4 + 3 and st == NST - 1):
                                e.matmul(p_[:, col:col + 128], l, r, start=s,
                                         stop=sp),
                                deps=dd,
                                inc=("te" if (h == grp * 4 + 3 and
                                              st == NST - 1) else None))
                            first = False
                    b.set_w(bk, ds_last[grp])
                su_toks = []
                for h in range(H):
                    bk = 2 + h // 4
                    tk = b.emit("vector", lambda e,
                                o=sb["sst"][:, h * 128:(h + 1) * 128],
                                p_=pb[bk][:, (h % 4) * 128:(h % 4 + 1) * 128],
                                gc=_gp(h, CH):
                                e.scalar_tensor_tensor(o, o, gc, p_, ALU.mult,
                                                       ALU.add),
                                deps=[ds_last[h // 4], sn, b.last("sst")],
                                inc="ve")
                    b.add_r(bk, tk)
                    su_toks.append(tk)
                b.note("sst", su_toks)
                gkf = f"gkf{c % 2}"
                kt = b.emit("gpsimd", lambda e,
                            d=dr["kvk"][c * 128:(c + 1) * 128, :]:
                            e.dma_start(out=d, in_=sb["kf"][:, :]),
                            deps=kf_toks + [(gkf, b.cnt.get(gkf, 0))],
                            inc=gkf, by=16)
                for hh in range(H):
                    b.note_add(("kfh", hh), kt)
                gvf = f"gvf{c % 2}"
                vt = b.emit("gpsimd", lambda e,
                            d=dr["kvv"][c * 128:(c + 1) * 128, :]:
                            e.dma_start(out=d, in_=sb["vf"][:, :]),
                            deps=v_toks + vg_toks +
                                 [(gvf, b.cnt.get(gvf, 0))],
                            inc=gvf, by=16)
                for j in range(NST * 2):
                    b.note_add(("vf", j), vt)
                    b.note(("ktmj", j), [ka_toks[j // 2], ds_last[0],
                                         ds_last[1]])
                ctx["kv"][c] = (kt, vt, sn)
                for fc in range(FC):
                    b.note_add(("fmh", "lnf", fc), [v_toks[-1], ds_last[1]])
                    b.note_add(("fml", "lnf", fc), [v_toks[-1], ds_last[1]])
                b.note(("rp", par), kro)
                for st in range(NST):
                    b.note(("xc", par * NST + st), lnf_toks[-1])
                if c + 1 < NCH:
                    ctx.setdefault("pre_ln", {})[c + 1] = emit_norm(
                        sb["xchunk"], ((c + 1) % 2) * NST * 1024,
                        ctx["mus"][c + 1],
                        [xld[c + 1], ds_last[0], ds_last[1]])
                if c + 2 < NCH:
                    load_chunk(c + 2)
                continue
            g_toks = proj_fm(f"wg{li}", sb["lnf"], lnf_all, sb["gfb"],
                             silu=True)
            for fc in range(FC):
                b.note_add(("fmh", "lnf", fc), g_toks[-1])
                b.note_add(("fml", "lnf", fc), g_toks[-1])
            qro = rope_tm(sb["qtm"], par, rld[c], q_toks, sign=+1)
            b.note(("rp", par), [kro, qro])
            qg_toks = []
            for st in range(NST):
                tk = b.emit("vector", lambda e,
                            o=sb["qtm"][:, st * 1024:(st + 1) * 1024],
                            g=sb["gq"][:, st * 1024:(st + 1) * 1024]:
                            e.tensor_mul(o, o, g),
                            deps=[CST, qro[st]], inc="ve")
                qg_toks.append(tk)
            qf_toks = []
            for h in range(H):
                bk = 4 + (h % 2)
                tt = None
                for st in range(NST):
                    ia = sb["qtm"][:, st * 1024 + h * 128:
                                   st * 1024 + (h + 1) * 128]
                    dd = [qg_toks[st], CST] + (b.bw(bk) if st == 0 else [])
                    tt = b.emit("tensor", lambda e,
                                o=pb[bk][:, st * 128:(st + 1) * 128], i=ia:
                                e.transpose(o, i, sb["ident"][:, :]),
                                deps=dd, inc=("te" if st == NST - 1 else None))
                b.set_w(bk, tt)
                tk = b.emit("vector", lambda e,
                            o=sb["qf"][:, h * CH:(h + 1) * CH],
                            i=pb[bk][:, 0:NST * 128]: e.tensor_copy(o, i),
                            deps=[tt, b.last(("qfh", h))], inc="ve")
                b.add_r(bk, tk)
                b.note(("qfh", h), tk)
                qf_toks.append(tk)
            ysb_toks = [None] * H
            a1_toks = [None] * H
            gst_last = None
            y2_last = None
            att_bk = (4, 0)
            y_bk = (5, 1)
            att_buf = (sb["attm"], sb["attm2"])

            def emit_att(h):
                bk = att_bk[h % 2]
                buf = att_buf[h % 2]
                am = b.last(("attm", h % 2))
                dd0 = [kf_toks[h], qf_toks[h]] + b.bw(bk)
                m0 = b.emit("tensor", lambda e, p_=pb[bk],
                            l=sb["kf"][:, h * CH:h * CH + 128],
                            r=sb["qf"][:, h * CH:(h + 1) * CH]:
                            e.matmul(p_[:, 0:CH], l, r, start=True,
                                     stop=False), deps=dd0)
                m1 = b.emit("tensor", lambda e, p_=pb[bk],
                            l=sb["kf"][:, h * CH + 128:h * CH + CH],
                            r=sb["qf"][:, h * CH + 128:(h + 1) * CH]:
                            e.matmul(p_[:, CH:CH + 128], l, r, start=False,
                                     stop=True), deps=[kf_toks[h], qf_toks[h]],
                            inc="te")
                b.set_w(bk, m1)
                e1 = b.emit("vector", lambda e, o=buf[:, 0:128],
                            p_=pb[bk]: e.tensor_mul(
                    o, p_[:, 0:128], sb["tri"][:, :]),
                    deps=[m1, CST, am], inc="ve")
                e2 = b.emit("vector", lambda e, o=buf[:, 128:256],
                            p_=pb[bk]: e.tensor_copy(o, p_[:, 128:256]),
                    deps=[m1, am], inc="ve")
                e3 = b.emit("vector", lambda e, o=buf[:, 256:384],
                            p_=pb[bk]: e.tensor_mul(
                    o, p_[:, 256:384], sb["tri"][:, :]),
                    deps=[m1, CST, am], inc="ve")
                for t in (e1, e2, e3):
                    b.add_r(bk, t)
                return (e1, e2, e3, buf)

            def emit_y(h, ectx):
                nonlocal y2_last
                e1, e2, e3, buf = ectx
                bk = y_bk[h % 2]
                dd = [b.last("sst"), qf_toks[h]] + b.bw(bk)
                y0 = b.emit("tensor", lambda e, p_=pb[bk],
                            l=sb["sst"][:, h * 128:(h + 1) * 128],
                            r=sb["qf"][:, h * CH:(h + 1) * CH]:
                            e.matmul(p_[:, 0:CH], l, r, start=True,
                                     stop=False), deps=dd)
                y1 = b.emit("tensor", lambda e, p_=pb[bk],
                            l=sb["vf"][:, h * 128:(h + 1) * 128],
                            r=buf[:, 0:CH]:
                            e.matmul(p_[:, 0:CH], l, r, start=False,
                                     stop=False),
                            deps=[e1, e2, v_toks[0], v_toks[1]])
                y2 = b.emit("tensor", lambda e, p_=pb[bk],
                            l=sb["vf"][:, 1024 + h * 128:1024 + (h + 1) * 128],
                            r=buf[:, 256:384]:
                            e.matmul(p_[:, 128:CH], l, r, start=False,
                                     stop=True),
                            deps=[e3, v_toks[2], v_toks[3]], inc="te")
                b.set_w(bk, y2)
                y2_last = y2
                b.note(("attm", h % 2), [e3, y2])
                ye = b.emit("scalar", lambda e, p_=pb[bk],
                            o=sb["ysb"][:, h * CH:(h + 1) * CH]:
                            e.copy(o, p_[:, 0:CH]),
                            deps=[y2, b.last(("ysb", h))], inc="sc")
                y2e = b.emit("scalar", lambda e, p_=pb[bk],
                             o=sb["fsc"][:, (h % 2) * CH:(h % 2 + 1) * CH]:
                             e.activation(o, p_[:, 0:CH], AF.Square),
                             deps=[y2, b.last("fsc"),
                                   b.last(("fsch", h % 2))], inc="sc")
                for t in (ye, y2e):
                    b.add_r(bk, t)
                ysb_toks[h] = ye
                return (ye, y2e)

            def emit_g(h, yctx):
                nonlocal gst_last
                ye, y2e = yctx
                dd = [ye, CST] + (b.bw(6) if h == 0 else [])
                g1 = b.emit("tensor", lambda e,
                            l=sb["head_sel"][:, h * 8:(h + 1) * 8],
                            r=sb["ysb"][:, h * CH:(h + 1) * CH],
                            s_=(h == 0):
                            e.matmul(pb[6][0:8, 0:CH], l, r, start=s_,
                                     stop=False), deps=dd)
                gst_last = b.emit("tensor", lambda e,
                                  l=sb["head_sel"][:, h * 8:(h + 1) * 8],
                                  r=sb["fsc"][:, (h % 2) * CH:
                                              (h % 2 + 1) * CH],
                                  sp_=(h == H - 1):
                                  e.matmul(pb[6][0:8, CH:2 * CH], l, r,
                                           start=False, stop=sp_),
                                  deps=[y2e, CST], inc="te")
                b.note(("fsch", h % 2), gst_last)
                b.note("fsc", gst_last)

            ectx_prev = None
            yctx_prev = None
            for h in range(H):
                cur = emit_att(h)
                if ectx_prev is not None:
                    ycur = emit_y(h - 1, ectx_prev)
                    if yctx_prev is not None:
                        emit_g(h - 2, yctx_prev)
                    yctx_prev = ycur
                ectx_prev = cur
            yctx_last = emit_y(H - 1, ectx_prev)
            emit_g(H - 2, yctx_prev)
            emit_g(H - 1, yctx_last)
            b.set_w(6, gst_last)
            g0 = sb["gstat"][0:8, 0:CH]
            g1_ = sb["gstat"][0:8, CH:2 * CH]
            g2 = sb["gstat"][0:8, 2 * CH:3 * CH]
            g3 = sb["gstat"][0:8, 3 * CH:4 * CH]
            gl = [b.last("gstat")] + [b.last(("hf", i)) for i in range(8)]
            s1_ = b.emit("vector", lambda e: e.tensor_scalar_mul(
                g0, pb[6][0:8, 0:CH], 1.0 / DH), deps=[gst_last, gl],
                inc="ve")
            s2_ = b.emit("vector", lambda e: e.tensor_scalar_mul(
                g1_, pb[6][0:8, CH:2 * CH], 1.0 / DH), deps=[gst_last, gl],
                inc="ve")
            for t in (s1_, s2_):
                b.add_r(6, t)
            s3_ = b.emit("vector", lambda e: e.tensor_tensor(
                g2, g0, g0, ALU.mult), deps=[s1_, gl], inc="ve")
            s4_ = b.emit("vector", lambda e: e.tensor_sub(g2, g1_, g2),
                         deps=[s2_, s3_], inc="ve")
            s5_ = b.emit("scalar", lambda e: e.activation(
                g2, g2, AF.Sqrt, bias=sb["epsc"][0:8, 0:1]),
                deps=[s4_, t_eps], inc="sc")
            s6_ = b.emit("vector", lambda e: e.reciprocal(g2, g2),
                         deps=[s5_], inc="ve")
            s7_ = b.emit("vector", lambda e: e.tensor_tensor(
                g3, g0, g2, ALU.mult), deps=[s6_, gl], inc="ve")
            gy_toks = []
            r2_last = None
            for h in range(H):
                bk = 4 + (h % 2)
                dd = [s6_, s7_, CST] + b.bw(bk)
                r1 = b.emit("tensor", lambda e,
                            l=sb["sel8"][0:8, h * 128:(h + 1) * 128], rr=g2,
                            p_=pb[bk][:, 0:CH]:
                            e.matmul(p_, l, rr, start=True,
                                     stop=False), deps=dd)
                r2 = b.emit("tensor", lambda e,
                            l=sb["sel8"][0:8, h * 128:(h + 1) * 128], rr=g3,
                            p_=pb[bk][:, CH:2 * CH]:
                            e.matmul(p_, l, rr, start=False,
                                     stop=True), deps=[s6_, s7_, CST],
                            inc="te")
                b.set_w(bk, r2)
                r2_last = r2
                a1 = b.emit("vector", lambda e,
                            o=sb["gyf"][:, h * CH:(h + 1) * CH],
                            i=sb["ysb"][:, h * CH:(h + 1) * CH],
                            p_=pb[bk][:, 0:CH]: e.tensor_mul(o, i, p_),
                            deps=[r2, ysb_toks[h], b.last(("gyf", h))],
                            inc="ve")
                a2 = b.emit("vector", lambda e,
                            o=sb["gyf"][:, h * CH:(h + 1) * CH],
                            p_=pb[bk][:, CH:2 * CH]: e.tensor_sub(o, o, p_),
                            deps=[a1], inc="ve")
                for t in (a1, a2):
                    b.add_r(bk, t)
                a3 = b.emit("vector", lambda e,
                            o=sb["gyf"][:, h * CH:(h + 1) * CH],
                            s=sb["gnsb"][:, li * 2 * FC + h:li * 2 * FC + h + 1],
                            bb=sb["gnsb"][:, li * 2 * FC + FC + h:
                                          li * 2 * FC + FC + h + 1]:
                            e.tensor_scalar(o, o, s, bb, ALU.mult, ALU.add),
                            deps=[a2, CST], inc="ve")
                a4 = b.emit("vector", lambda e,
                            o=sb["gyf"][:, h * CH:(h + 1) * CH],
                            g=sb["gfb"][:, h * CH:(h + 1) * CH]:
                            e.tensor_mul(o, o, g),
                            deps=[a3, g_toks[h // 2]], inc="ve")
                a1_toks[h] = a1
                b.note(("ysb", h), [a1, gst_last])
                b.note(("gyf", h), a4)
                gy_toks.append(a4)
            b.note("gstat", [s7_, r2_last])
            for i in range(8):
                b.note_add(("hf", i), [s7_, r2_last])
            for bk in range(4):
                b.note_add(("pf", id(sb["gfb"]), bk), gy_toks[-1])
            if mode == "merged":
                ds_last = {}
                for grp in range(2):
                    bk = 2 + grp
                    first = True
                    for h in range(grp * 4, grp * 4 + 4):
                        for st in range(NST):
                            dd = [kro[st], vg_toks[st]]
                            if first:
                                dd += b.bw(bk)
                            ds_last[grp] = b.emit(
                                "tensor", lambda e, p_=pb[bk],
                                l=sb["ktm"][:, st * 1024 + h * 128:
                                            st * 1024 + (h + 1) * 128],
                                r=sb["lnt"][:, st * 1024 + h * 128:
                                            st * 1024 + (h + 1) * 128],
                                s=first, col=(h % 4) * 128,
                                sp=(h == grp * 4 + 3 and st == NST - 1):
                                e.matmul(p_[:, col:col + 128], l, r, start=s,
                                         stop=sp),
                                deps=dd,
                                inc=("te" if (h == grp * 4 + 3 and st == NST - 1)
                                     else None))
                            first = False
                    b.set_w(bk, ds_last[grp])
                for j in range(NST * 2):
                    b.note(("ktmj", j), [ka_toks[j // 2], ds_last[0],
                                         ds_last[1]])
                    b.note(("qtmj", j), qf_toks[-1])
                    b.note_add(("vf", j), [vg_toks[j // 2], y2_last])
                for h in range(H):
                    b.note_add(("kfh", h), y2_last)
                    b.note_add(("qfh", h), y2_last)
                su_toks = []
                for h in range(H):
                    bk = 2 + h // 4
                    tk = b.emit("vector", lambda e,
                                o=sb["sst"][:, h * 128:(h + 1) * 128],
                                p_=pb[bk][:, (h % 4) * 128:(h % 4 + 1) * 128],
                                gc=_gp(h, CH):
                                e.scalar_tensor_tensor(o, o, gc, p_, ALU.mult,
                                                       ALU.add),
                                deps=[ds_last[h // 4], b.last("sst")], inc="ve")
                    b.add_r(bk, tk)
                    su_toks.append(tk)
                b.note("sst", su_toks)
            else:
                ds_last = {0: None, 1: None}
            gsp = []
            for half2 in range(2):
                sl = slice(half2 * 1024, (half2 + 1) * 1024)
                th = b.emit("scalar", lambda e, o=sb["gyf_h"][:, sl],
                            i=sb["gyf"][:, sl]: e.copy(o, i),
                            deps=[gy_toks, b.last(("gysp", half2)),
                                  b.last("qwt")], inc="sc")
                tl = b.emit("vector", lambda e, o=sb["gyf_l"][:, sl],
                            i=sb["gyf"][:, sl], h_=sb["gyf_h"][:, sl]:
                            e.tensor_sub(o, i, h_), deps=[th], inc="ve")
                b.note(("gysp", half2), [th, tl])
                b.note_add("qwtv", [th, tl])
                gsp.append([th, tl])
            gy_all = b.emit("tensor", lambda e: e.nop(), deps=gsp, inc="te")
            wo_mm = [None] * (NST * 2)
            wo_banks = [4, 5, 0, 1]
            for kk in range(0, FC, 2):
              ss_, wt2, wtok = wload_sp2((f"wo{li}", kk, None),
                                         (f"wo{li}", kk + 1, None))
              for dk in range(2):
                k = kk + dk
                wt = wt2[:, dk * 2048:(dk + 1) * 2048]
                for p, (xs, ws) in enumerate(((0, 0), (0, 1), (1, 0))):
                    lsrc = sb["gyf_h"] if xs == 0 else sb["gyf_l"]
                    for j in range(NST * 2):
                        st, half = divmod(j, 2)
                        bk = wo_banks[j]
                        dd = [wtok, gy_all]
                        if k == 0 and p == 0:
                            dd += b.bw(bk)
                        wo_mm[j] = b.emit(
                            "tensor", lambda e, p_=pb[bk],
                            l=lsrc[:, k * CH + st * 128:k * CH + (st + 1) * 128],
                            r=wt[:, ws * 1024 + half * 512:
                                 ws * 1024 + (half + 1) * 512],
                            s=(k == 0 and p == 0),
                            sp=(k == FC - 1 and p == 2):
                            e.matmul(p_[:, 0:512], l, r, start=s, stop=sp),
                            deps=dd,
                            inc=("te" if ((k == FC - 1 and p == 2)
                                          or (p == 2 and j == NST * 2 - 1))
                                 else None))
              wfree2(ss_, wo_mm[-1])
            for h in range(H):
                b.note_add(("gyf", h), wo_mm[-1])
            yr_toks = []
            for j in range(NST * 2):
                st, half = divmod(j, 2)
                bk = wo_banks[j]
                b.set_w(bk, wo_mm[j])
                slotid = par * NST + st
                xa = sb["xchunk"][:, slotid * 1024 + half * 512:
                                  slotid * 1024 + (half + 1) * 512]
                tk = b.emit("vector", lambda e,
                            o=sb["yr"][:, st * 1024 + half * 512:
                                       st * 1024 + (half + 1) * 512],
                            i=pb[bk][:, 0:512], x_=xa: e.tensor_add(o, i, x_),
                            deps=[wo_mm[j], xld[c][st], b.last(("yr", j))],
                            inc="ve")
                b.add_r(bk, tk)
                yr_toks.append(tk)
            for st in range(NST):
                b.note(("xc", par * NST + st),
                       [yr_toks[2 * st], yr_toks[2 * st + 1]])
            if c + 2 < NCH:
                load_chunk(c + 2)
            mus2 = []
            for st in range(NST):
                ya = sb["yr"][:, st * 1024:(st + 1) * 1024]
                mu = sb["st2"][:, st:st + 1]
                rr = sb["st2"][:, NST + st:NST + st + 1]
                rdy = stats_pass(ya, mu, rr,
                                 [yr_toks[2 * st], yr_toks[2 * st + 1]],
                                 ("i", st))
                mus2.append((mu, rr, rdy, ("i", st)))
            def mk_mm_w1(mm_bank, lastk):
                def mm_fc(fc, tlo, wt, wtok):
                    for p, (xs, ws) in enumerate(((0, 0), (0, 1), (1, 0))):
                        rsrc = sb["ln2f_h"] if xs == 0 else sb["ln2f_l"]
                        for j in range(8):
                            bk = j // 2
                            dd = [wtok, tlo]
                            if fc == 0 and p == 0 and j % 2 == 0:
                                dd += b.bw(bk)
                            want_tok = (fc == FC - 1 and p == 2 and j % 2 == 1)
                            mm = b.emit(
                                "tensor", lambda e,
                                p_=pb[bk][:, (j % 2) * CH:(j % 2 + 1) * CH],
                                l=wt[:, ws * 1024 + j * 128:
                                     ws * 1024 + (j + 1) * 128],
                                r=rsrc[:, fc * CH:(fc + 1) * CH],
                                s=(fc == 0 and p == 0 and j % 2 == 0),
                                sp=(fc == FC - 1 and p == 2 and j % 2 == 1):
                                e.matmul(p_, l, r, start=s, stop=sp),
                                deps=dd,
                                inc=("te" if (want_tok or (p == 2 and j == 7))
                                     else None))
                            if want_tok:
                                mm_bank[bk] = mm
                            if p == 2 and j == 7:
                                lastk[0] = mm
                    return lastk[0]
                return mm_fc

            mm_bank = [None] * 4
            lastk = [None]
            lnf2_toks = ln_proj_fused(
                sb["yr"], 0, mus2, "ln2f", [yr_toks],
                [((f"w1{li}", kk, 0), (f"w1{li}", kk + 1, 0))
                 for kk in range(0, FC, 2)],
                mk_mm_w1(mm_bank, lastk))
            ln2_all = b.emit("tensor", lambda e: e.nop(), deps=lnf2_toks,
                             inc="tnp")
            if mode == "B" and c + 1 < NCH:
                ctx.setdefault("pre_ln", {})[c + 1] = emit_norm(
                    sb["xchunk"], ((c + 1) % 2) * NST * 1024,
                    ctx["mus"][c + 1], [xld[c + 1], lnf2_toks[-1]])
            h_toks = []
            for rnd in range(2):
                if rnd == 1:
                  mm_bank = [None] * 4
                  for kk in range(0, FC, 2):
                    ss_, wt2, wtok = wload_sp2((f"w1{li}", kk, rnd),
                                               (f"w1{li}", kk + 1, rnd))
                    for dk in range(2):
                      k = kk + dk
                      wt = wt2[:, dk * 2048:(dk + 1) * 2048]
                      last_of_k = None
                      for p, (xs, ws) in enumerate(((0, 0), (0, 1), (1, 0))):
                        rsrc = sb["ln2f_h"] if xs == 0 else sb["ln2f_l"]
                        for j in range(8):
                            bk = j // 2
                            dd = [wtok, ln2_all]
                            if k == 0 and p == 0 and j % 2 == 0:
                                dd += b.bw(bk)
                            want_tok = (k == FC - 1 and p == 2 and j % 2 == 1)
                            mm = b.emit(
                                "tensor", lambda e,
                                p_=pb[bk][:, (j % 2) * CH:(j % 2 + 1) * CH],
                                l=wt[:, ws * 1024 + j * 128:
                                     ws * 1024 + (j + 1) * 128],
                                r=rsrc[:, k * CH:(k + 1) * CH],
                                s=(k == 0 and p == 0 and j % 2 == 0),
                                sp=(k == FC - 1 and p == 2 and j % 2 == 1):
                                e.matmul(p_, l, r, start=s, stop=sp),
                                deps=dd,
                                inc=("te" if (want_tok or (p == 2 and j == 7))
                                     else None))
                            if want_tok:
                                mm_bank[bk] = mm
                            if p == 2 and j == 7:
                                last_of_k = mm
                    wfree2(ss_, last_of_k)
                for bk in range(4):
                    b.set_w(bk, mm_bank[bk])
                    hcol = (rnd * 8 + bk * 2) * CH
                    t32 = b.emit("scalar", lambda e,
                                 o=sb["lnf"][:, bk * 512:(bk + 1) * 512],
                                 i=pb[bk][:, 0:512]:
                                 e.activation(o, i, AF.Gelu_apprx_tanh),
                                 deps=[mm_bank[bk], b.last(("g32", bk)),
                                       b.last(("fm32", bk * 2)),
                                       b.last(("fm32", bk * 2 + 1))],
                                 inc="sc")
                    b.note(("fm32", bk * 2), t32)
                    b.note(("fm32", bk * 2 + 1), t32)
                    b.add_r(bk, t32)
                    th = b.emit("scalar", lambda e,
                                o=sb["hf_h"][:, hcol:hcol + 512],
                                i=sb["lnf"][:, bk * 512:(bk + 1) * 512]:
                                e.copy(o, i),
                                deps=[t32, b.last(("hf", rnd * 4 + bk))],
                                inc="sc")
                    tl = b.emit("vector", lambda e,
                                o=sb["hf_l"][:, hcol:hcol + 512],
                                i=sb["lnf"][:, bk * 512:(bk + 1) * 512],
                                h_=sb["hf_h"][:, hcol:hcol + 512]:
                                e.tensor_sub(o, i, h_),
                                deps=[th], inc="ve")
                    b.note(("g32", bk), tl)
                    b.note(("hf", rnd * 4 + bk), [th, tl])
                    h_toks.append([th, tl])
            for fc in range(FC):
                b.note_add(("fmh", "ln2f", fc), h_toks[-1])
                b.note_add(("fml", "ln2f", fc), h_toks[-1])
            hf_all = b.emit("tensor", lambda e: e.nop(), deps=h_toks,
                            inc="tnp")
            w2_mm = [None] * (NST * 2)
            for kk in range(0, HC, 2):
              ss_, wt2, wtok = wload_sp2((f"w2{li}", kk, None),
                                         (f"w2{li}", kk + 1, None))
              for dk in range(2):
                k = kk + dk
                wt = wt2[:, dk * 2048:(dk + 1) * 2048]
                for p, (xs, ws) in enumerate(((0, 0), (0, 1), (1, 0))):
                    lsrc = sb["hf_h"] if xs == 0 else sb["hf_l"]
                    for j in range(NST * 2):
                        st, half = divmod(j, 2)
                        bk = wo_banks[j]
                        dd = [wtok, h_toks[(k // 8) * 4 + (k % 8) // 2]]
                        if k == 0 and p == 0:
                            dd += b.bw(bk)
                        w2_mm[j] = b.emit(
                            "tensor", lambda e, p_=pb[bk],
                            l=lsrc[:, k * CH + st * 128:k * CH + (st + 1) * 128],
                            r=wt[:, ws * 1024 + half * 512:
                                 ws * 1024 + (half + 1) * 512],
                            s=(k == 0 and p == 0),
                            sp=(k == HC - 1 and p == 2):
                            e.matmul(p_[:, 0:512], l, r, start=s, stop=sp),
                            deps=dd,
                            inc=("te" if ((k == HC - 1 and p == 2)
                                          or (p == 2 and j == NST * 2 - 1))
                                 else None))
              wfree2(ss_, w2_mm[-1])
            for i in range(8):
                b.note_add(("hf", i), w2_mm[-1])
            st_toks = {0: [], 1: []}
            for j in range(NST * 2):
                st, half = divmod(j, 2)
                bk = wo_banks[j]
                b.set_w(bk, w2_mm[j])
                tk = b.emit("vector", lambda e,
                            o=sb["ysb"][:, st * 1024 + half * 512:
                                        st * 1024 + (half + 1) * 512],
                            i=pb[bk][:, 0:512],
                            y_=sb["yr"][:, st * 1024 + half * 512:
                                        st * 1024 + (half + 1) * 512]:
                            e.tensor_add(o, i, y_),
                            deps=[w2_mm[j], yr_toks[j],
                                  b.last(("ysb", 2 * j)),
                                  b.last(("ysb", 2 * j + 1)),
                                  ds_last[0], ds_last[1]],
                            inc="ve")
                b.add_r(bk, tk)
                b.note(("yr", j), tk)
                st_toks[st].append(tk)
            for st in range(NST):
                t0 = c * CH + st * 128
                tt_i = c * NST + st
                gs_name = f"gst{tt_i % 4}"
                drain = [(gs_name, b.cnt.get(gs_name, 0))]
                tk = b.emit("gpsimd", lambda e,
                            d=dr[xout][t0:t0 + 128, :],
                            s=sb["ysb"][:, st * 1024:(st + 1) * 1024]:
                            e.dma_start(out=d, in_=s),
                            deps=st_toks[st] + war_marks(c * NST + st) + drain,
                            inc=gs_name, by=16)
                for hh in range(st * 4, st * 4 + 4):
                    b.note(("ysb", hh), tk)
                out_marks[c * NST + st] = tk
        return out_marks, xld

    def emit_state_exchange(ctx):
        sinm_v = sb["ktm"].ap()[:, 1024:2048]
        t1 = b.emit("vector", lambda e: e.tensor_scalar(
            sb["lnt"][:, 0:1024], sb["sst"][:, :], sb["cmask"][:, 0:1], None,
            ALU.mult), deps=[b.last("sst"), CST, b.last(("lnt", 0))],
            inc="ve")
        b.note(("lnt", 0), t1)
        ts_ = b.emit("gpsimd", lambda e: e.dma_start(
            out=dr["bst_i"][:, :], in_=sb["lnt"][:, 0:1024]),
            deps=[t1, b.last("gbst")], inc="gbst", by=16)
        if collective:
            groups = ([[i, i + 1] for i in range(0, n_cores, 2)]
                      if n_cores > 1 else [[0]])
            cc = b.emit("gpsimd", lambda e, g=groups: e.collective_compute(
                "AllReduce", ALU.add, replica_groups=g,
                ins=[dr["bst_i"].ap().opt()], outs=[dr["bst_o"].ap().opt()]),
                deps=[ts_], inc="cc")
            b.note("gbst", ts_)
        else:
            cc = b.emit("gpsimd", lambda e: e.dma_start(
                out=dr["bst_o"][:, :], in_=dr["bst_i"][:, :]),
                deps=[ts_], inc="gbst", by=16)
            b.note("gbst", cc)
        ld = b.emit("sync", lambda e: e.dma_start(
            out=sinm_v, in_=dr["bst_o"][:, :]),
            deps=[cc, b.last(("ktmj", 2)), b.last(("ktmj", 3)),
                  ("ldsm", b.cnt.get("ldsm", 0))],
            inc="ldsm", by=16)
        um = b.emit("vector", lambda e: e.tensor_scalar(
            sinm_v, sinm_v, sb["cmask"][:, 1:2], None, ALU.mult),
            deps=[ld, CST], inc="ve")
        b.note_add(("ktmj", 2), um)
        b.note_add(("ktmj", 3), um)
        b.note("sst", t1)
        ctx["sin_tok"] = um

    def emit_ln_pass(src_dram, dst_dram, rd_marks, war_marks):
        out_marks = {}
        for tt in range(NT):
            par = tt % 4
            xa = sb["xchunk"][:, par * 1024:(par + 1) * 1024]
            xtok = b.emit("sync", lambda e, d=xa,
                          s=dr[src_dram][tt * 128:(tt + 1) * 128, :]:
                          e.dma_start(out=d, in_=s),
                          deps=[b.last(("xc", par))] + rd_marks(tt),
                          inc=f"ldx{par}", by=16)
            mu = sb["stats"][:, tt:tt + 1]
            rr = sb["stats"][:, NT + tt:NT + tt + 1]
            ts_ = stats_pass(xa, mu, rr, [xtok], tt)
            tln = b.emit("vector", lambda e, i=xa, m=mu, r=rr: e.tensor_scalar(
                sb["lnt"][:, 0:1024], i, m, r, ALU.subtract, ALU.mult),
                deps=[ts_, xtok, b.last(("lnt", 0))], inc="ve")
            b.note(("xc", par), tln)
            b.note_add(("str", tt), tln)
            gs_name = f"gst{tt % 4}"
            drain = [(gs_name, b.cnt.get(gs_name, 0))]
            stok = b.emit("gpsimd", lambda e,
                          d=dr[dst_dram][tt * 128:(tt + 1) * 128, :],
                          s=sb["lnt"][:, 0:1024]: e.dma_start(out=d, in_=s),
                          deps=[tln] + war_marks(tt) + drain,
                          inc=gs_name, by=16)
            b.note(("lnt", 0), stok)
            out_marks[tt] = stok
        return out_marks

    def emit_merge(srcA, srcB, dst_dram, rdA, rdB, war_marks):
        out_marks = {}
        for tt in range(NT):
            par = tt % 2
            xa = sb["xchunk"][:, par * 1024:(par + 1) * 1024]
            xbslot = 2 + par
            xbv = sb["xchunk"][:, xbslot * 1024:(xbslot + 1) * 1024]
            atok = b.emit("sync", lambda e, d=xa,
                          s=dr[srcA][tt * 128:(tt + 1) * 128, :]:
                          e.dma_start(out=d, in_=s),
                          deps=[b.last(("xc", par))] + rdA(tt),
                          inc=f"ldx{par}", by=16)
            btok = b.emit("sync", lambda e, d=xbv,
                          s=dr[srcB][tt * 128:(tt + 1) * 128, :]:
                          e.dma_start(out=d, in_=s),
                          deps=[b.last(("xc", xbslot))] + rdB(tt),
                          inc=f"ldx{xbslot}", by=16)
            tadd = b.emit("vector", lambda e, a=xa, bb=xbv:
                          e.tensor_add(sb["lnt"][:, 0:1024], a, bb),
                          deps=[atok, btok, b.last(("lnt", 0))], inc="ve")
            b.note(("xc", par), tadd)
            b.note(("xc", xbslot), tadd)
            gs_name = f"gst{tt % 4}"
            drain = [(gs_name, b.cnt.get(gs_name, 0))]
            stok = b.emit("gpsimd", lambda e,
                          d=dr[dst_dram][tt * 128:(tt + 1) * 128, :],
                          s=sb["lnt"][:, 0:1024]: e.dma_start(out=d, in_=s),
                          deps=[tadd] + war_marks(tt) + drain,
                          inc=gs_name, by=16)
            b.note(("lnt", 0), stok)
            out_marks[tt] = stok
        return out_marks

    wsg = emit_wsplit_gen()
    qw_tok, ws_tok = emit_qw_prep()
    for _ in wsg:
        pass

    def mk_rd(mk):
        return lambda tt: [mk[tt]] if tt in mk else []

    def mk_war(reads):
        rmap = {}
        for cc, toks in reads.items():
            for st_, tok in enumerate(toks):
                rmap[cc * NST + st_] = tok
        return lambda tt: [rmap[tt]] if tt in rmap else []

    none_m = lambda tt: []
    m_x1 = emit_ffn_outer("x_in", "xb0", qw_tok, ws_tok, none_m, none_m)
    b.note("qwt", [m_x1[NT - 1], ("qst", b.cnt.get("qst", 0))])
    m_s0 = emit_ln_pass("xb0", "xb1", mk_rd(m_x1), none_m)
    marks = m_s0
    reads = {}
    war_prev = none_m
    buf_in, buf_out = "xb1", "xb2"
    for li in range(depth):
        if seq_split:
            ctx = {}
            emit_layer(li, buf_in, None, mk_rd(marks), none_m, mode="A",
                       ctx=ctx)
            emit_state_exchange(ctx)
            marks, xld = emit_layer(li, buf_in, buf_out, mk_rd(marks),
                                    war_prev, mode="B", ctx=ctx)
        else:
            marks, xld = emit_layer(li, buf_in, buf_out, mk_rd(marks),
                                    war_prev)
        war_prev = mk_war(xld)
        reads = xld
        buf_in, buf_out = buf_out, buf_in
    m_x2 = emit_merge("xb0", buf_in, buf_out, mk_rd(m_x1), mk_rd(marks),
                      war_prev)
    emit_ffn_outer(buf_out, "out", None, ws_tok, mk_rd(m_x2), none_m,
                   qw_loader=emit_qw_reload)
    fin = [(f"gst{i}", b.cnt.get(f"gst{i}", 0)) for i in range(4)
           if b.cnt.get(f"gst{i}", 0) > 0]
    if b.cnt.get("gbc", 0):
        fin.append(("gbc", b.cnt["gbc"]))
    b.wait("gpsimd", *fin)

    with nc.Block() as block:
        @block.sync
        def _(e):
            for f in b.stream["sync"]:
                f(e)

        @block.scalar
        def _(e):
            for f in b.stream["scalar"]:
                f(e)

        @block.vector
        def _(e):
            for f in b.stream["vector"]:
                f(e)

        @block.tensor
        def _(e):
            for f in b.stream["tensor"]:
                f(e)

        @block.gpsimd
        def _(e):
            for f in b.stream["gpsimd"]:
                f(e)

    return nc


def _prep_inputs(inputs, T, depth, n_cores=8, seq_split=False):
    perm = np.zeros(D, dtype=np.int64)
    for h in range(H):
        base = h * DH
        perm[base:base + 64] = base + np.arange(0, DH, 2)
        perm[base + 64:base + DH] = base + np.arange(1, DH, 2)
    consts = host_constants()
    cos_t, sin_t = rope_tables(0, T)
    ln_sb = np.concatenate(
        [np.asarray(inputs["L0_attn_ln_scale"], np.float32).reshape(FC, 128).T,
         np.asarray(inputs["L0_attn_ln_bias"], np.float32).reshape(FC, 128).T],
        axis=1)
    gn = np.asarray(inputs["L0_attn_gn_scale"], np.float32)
    gnb = np.asarray(inputs["L0_attn_gn_bias"], np.float32)
    gn_sb = np.concatenate(
        [np.concatenate([gn[li].reshape(FC, 128).T,
                         gnb[li].reshape(FC, 128).T], axis=1)
         for li in range(depth)], axis=1)
    fbias = np.broadcast_to(
        np.asarray(inputs["L0_ffn_bias"], np.float32)[None, :], (128, D)).copy()
    shared = {
        "fw": np.ascontiguousarray(np.asarray(inputs["L0_ffn_weight"],
                                              np.float32)),
        "fbias": fbias, "ln_sb": np.ascontiguousarray(ln_sb),
        "gn_sb": np.ascontiguousarray(gn_sb),
        "rope_c": cos_t, "rope_s": sin_t,
        "gq_t": consts["gq_t"], "gka_t": consts["gka_t"],
        "gsf_t": consts["gsf_t"], "tri": consts["tri"],
        "ident": consts["ident"], "ones1": consts["ones1"],
        "sel8": consts["sel8"], "head_sel": consts["head_sel"],
    }
    for li in range(depth):
        shared[f"wq{li}"] = np.ascontiguousarray(
            np.asarray(inputs["L0_attn_Wq"][li], np.float32)[:, perm])
        shared[f"wk{li}"] = np.ascontiguousarray(
            np.asarray(inputs["L0_attn_Wk"][li], np.float32)[:, perm])
        shared[f"wv{li}"] = np.ascontiguousarray(
            np.asarray(inputs["L0_attn_Wv"][li], np.float32))
        shared[f"wg{li}"] = np.ascontiguousarray(
            np.asarray(inputs["L0_attn_Wg"][li], np.float32))
        shared[f"wo{li}"] = np.ascontiguousarray(
            np.asarray(inputs["L0_attn_Wo"][li], np.float32))
        shared[f"w1{li}"] = np.ascontiguousarray(
            np.asarray(inputs["L0_attn_W1"][li], np.float32))
        shared[f"w2{li}"] = np.ascontiguousarray(
            np.asarray(inputs["L0_attn_W2"][li], np.float32))
    X = np.asarray(inputs["L0_attn_X"], np.float32)
    in_maps = []
    for c in range(n_cores):
        m = dict(shared)
        if seq_split:
            u = c % 2
            m["x_in"] = np.ascontiguousarray(X[c // 2, u * T:(u + 1) * T])
            cs, sn = rope_tables(u * T, T)
            m["rope_c"], m["rope_s"] = cs, sn
            cmask = np.zeros((128, 2), np.float32)
            cmask[:, 0] = 1.0 if u == 0 else 0.0
            cmask[:, 1] = 0.0 if u == 0 else 1.0
            m["cmask"] = cmask
        else:
            m["x_in"] = np.ascontiguousarray(X[c // 2])
            m["cmask"] = np.zeros((128, 2), np.float32)
        in_maps.append(m)
    return in_maps


def kernel(**inputs):
    T = 4096
    nc = build_program(T, depth=DEPTH, n_cores=8, collective=True,
                       seq_split=True)
    in_maps = _prep_inputs(inputs, T, DEPTH, seq_split=True)
    res = run_bass_kernel_spmd(nc, in_maps, core_ids=list(range(8)))
    out = np.stack(
        [np.concatenate([res.results[2 * bq]["out"],
                         res.results[2 * bq + 1]["out"]], axis=0)
         for bq in range(4)], axis=0)
    return np.ascontiguousarray(out.astype(np.float32))


# version 24
# speedup vs baseline: 1.0137x; 1.0137x over previous
import numpy as np

import concourse.bass as bass
import concourse.mybir as mybir
from concourse.bass_utils import run_bass_kernel_spmd

F32 = mybir.dt.float32
BF16 = mybir.dt.bfloat16
AF = mybir.ActivationFunctionType
ALU = mybir.AluOpType
AX = mybir.AxisListType

D = 1024
H = 8
DH = 128
F = 2048
DEPTH = 4
EPS = 1e-5
FC = D // 128
HC = F // 128
CH = 256
NST = CH // 128

GAMMAS = 1.0 - np.exp(np.linspace(np.log(1 / 32), np.log(1 / 512), H))


def _gp(h, e):
    v = float(GAMMAS[h]) ** e
    return 0.0 if abs(v) < 1e-37 else v


def host_constants():
    c = {}
    p = np.arange(128, dtype=np.float64)
    gq = np.zeros((128, NST * 1024), dtype=np.float32)
    gka = np.zeros((128, NST * 1024), dtype=np.float32)
    for st in range(NST):
        for h in range(H):
            rr = st * 128 + p
            gq[:, st * 1024 + h * 128:st * 1024 + (h + 1) * 128] = \
                (GAMMAS[h] ** (rr + 1)).astype(np.float32)[:, None]
            gka[:, st * 1024 + h * 128:st * 1024 + (h + 1) * 128] = \
                (GAMMAS[h] ** (-(rr + 1))).astype(np.float32)[:, None]
    c["gq_t"] = gq
    c["gka_t"] = gka
    gsf = np.zeros((128, NST * 1024), dtype=np.float32)
    for st in range(NST):
        for h in range(H):
            col = (GAMMAS[h] ** (CH - 1 - (st * 128 + p))).astype(np.float32)
            gsf[:, st * 1024 + h * 128:st * 1024 + (h + 1) * 128] = col[:, None]
    c["gsf_t"] = gsf
    m = np.arange(128)
    c["tri"] = (m[None, :] >= m[:, None]).astype(np.float32)
    c["ident"] = np.eye(128, dtype=np.float32)
    c["ones1"] = np.ones((1, 128), dtype=np.float32)
    hs = np.zeros((128, H * 8), dtype=np.float32)
    for h in range(H):
        hs[:, h * 8 + h] = 1.0
    c["head_sel"] = hs
    s8 = np.zeros((8, H * 128), dtype=np.float32)
    for h in range(H):
        s8[h, h * 128:(h + 1) * 128] = 1.0
    c["sel8"] = s8
    return c


def rope_tables(pos0, T):
    inv = 10000.0 ** (-np.arange(0, DH, 2, dtype=np.float64) / DH)
    pos = np.arange(pos0, pos0 + T, dtype=np.float64)
    ang = pos[:, None] * inv[None, :]
    return np.cos(ang).astype(np.float32), np.sin(ang).astype(np.float32)


ENGS = ("sync", "scalar", "vector", "tensor", "gpsimd")


class B:
    def __init__(self, nc):
        self.nc = nc
        self.stream = {e: [] for e in ENGS}
        self.sems = {}
        self.cnt = {}
        self.bank_w = {}
        self.bank_r = {}
        self.lastd = {}
        self.waited = {e: {} for e in ENGS}
        self.pos = {e: 0 for e in ENGS}
        self.prod = {}

    def sem(self, n):
        if n not in self.sems:
            self.sems[n] = self.nc.alloc_semaphore(name=n)
            self.cnt[n] = 0
        return self.sems[n]

    def _flat(self, deps, out=None):
        if out is None:
            out = []
        for t in deps:
            if t is None:
                continue
            if isinstance(t, list):
                self._flat(t, out)
            else:
                out.append(t)
        return out

    def _filter_waits(self, eng, dd):
        ws = []
        n = self.pos[eng]
        for s, v in dd.items():
            if v <= 0:
                continue
            if v <= self.waited[eng].get(s, 0):
                continue
            pr = self.prod.get(s)
            if pr is not None and pr[0] == "tensor" and eng == "tensor":
                continue
            ws.append((s, v))
        return ws

    def emit(self, eng, build, deps=(), inc=None, by=1):
        dd = {}
        for s, v in self._flat(deps):
            dd[s] = max(dd.get(s, 0), v)
        ws_names = self._filter_waits(eng, dd)
        for s, v in ws_names:
            self.waited[eng][s] = v
        ws = [(self.sem(s), v) for s, v in ws_names]
        tok = None
        it = None
        my_pos = self.pos[eng] + len(ws)
        if inc is not None:
            self.sem(inc)
            self.cnt[inc] += by
            tok = (inc, self.cnt[inc])
            it = (self.sems[inc], by)
            pr = self.prod.get(inc)
            if pr is None:
                self.prod[inc] = (eng, {self.cnt[inc]: my_pos})
            elif pr[0] == eng:
                pr[1][self.cnt[inc]] = my_pos
            else:
                self.prod[inc] = (None, {})

        def run(e, ws=ws, build=build, it=it):
            for s, v in ws:
                e.wait_ge(s, v)
            inst = build(e)
            if it is not None:
                inst.then_inc(it[0], it[1])

        self.stream[eng].append(run)
        self.pos[eng] = my_pos + 1
        return tok

    def wait(self, eng, *deps):
        dd = {}
        for s, v in self._flat(list(deps)):
            dd[s] = max(dd.get(s, 0), v)
        ws_names = self._filter_waits(eng, dd)
        for s, v in ws_names:
            self.waited[eng][s] = v
        ws = [(self.sem(s), v) for s, v in ws_names]

        def run(e, ws=ws):
            for s, v in ws:
                e.wait_ge(s, v)

        self.stream[eng].append(run)
        self.pos[eng] += len(ws)

    def bw(self, bk):
        return [self.bank_w.get(bk)] + list(self.bank_r.get(bk, []))

    def set_w(self, bk, tok):
        self.bank_w[bk] = tok
        self.bank_r[bk] = []

    def add_r(self, bk, tok):
        self.bank_r.setdefault(bk, []).append(tok)

    def last(self, key):
        return self.lastd.get(key)

    def note(self, key, tok):
        self.lastd[key] = tok

    def note_add(self, key, tok):
        cur = self.lastd.get(key)
        if cur is None:
            self.lastd[key] = [tok]
        elif isinstance(cur, list):
            cur.append(tok)
        else:
            self.lastd[key] = [cur, tok]


def build_program(T, depth=DEPTH, n_cores=8, collective=True,
                  seq_split=False):
    assert T % CH == 0
    NCH = T // CH
    NT = T // 128
    nc = bass.Bass("TRN2", target_bir_lowering=False)
    b = B(nc)

    dr = {}

    def dram(name, shape, dtype=F32, kind="ExternalInput"):
        dr[name] = nc.dram_tensor(name, shape, dtype, kind=kind)

    dram("x_in", [T, D])
    dram("out", [T, D], kind="ExternalOutput")
    for li in range(depth):
        for w, shp in (("wq", [D, D]), ("wk", [D, D]), ("wv", [D, D]),
                       ("wg", [D, D]), ("wo", [D, D]), ("w1", [D, F]),
                       ("w2", [F, D])):
            dram(f"{w}{li}", shp)
    dram("fw", [D, D])
    dram("fbias", [128, D])
    dram("ln_sb", [128, 2 * FC])
    dram("gn_sb", [128, depth * 2 * FC])
    dram("rope_c", [T, 64])
    dram("rope_s", [T, 64])
    dram("gq_t", [128, NST * 1024])
    dram("gka_t", [128, NST * 1024])
    dram("gsf_t", [128, NST * 1024])
    dram("tri", [128, 128])
    dram("ident", [128, 128])
    dram("ones1", [1, 128])
    dram("sel8", [8, H * 128])
    dram("head_sel", [128, H * 8])
    dram("xb0", [T, D], kind="Internal")
    dram("xb1", [T, D], kind="Internal")
    dram("xb2", [T, D], kind="Internal")
    dram("bmax_i", [128, 4], kind="Internal")
    dram("bmax_o", [128, 4], kind="Internal")
    dram("kvk", [(T // CH) * 128, H * CH], kind="Internal")
    dram("kvv", [(T // CH) * 128, NST * 1024], kind="Internal")
    dram("snapb", [(T // CH) * 128, H * DH], kind="Internal")
    dram("bst_i", [128, H * DH], kind="Internal")
    dram("bst_o", [128, H * DH], kind="Internal")
    dram("cmask", [128, 2])
    wsp_keys = []
    for li_ in range(depth):
        for nm in ("wk", "wv", "wq", "wg"):
            for k_ in range(FC):
                wsp_keys.append((f"{nm}{li_}", k_, None))
        for k_ in range(FC):
            wsp_keys.append((f"wo{li_}", k_, None))
        for hf_ in range(2):
            for k_ in range(FC):
                wsp_keys.append((f"w1{li_}", k_, hf_))
        for k_ in range(F // 128):
            wsp_keys.append((f"w2{li_}", k_, None))
    wsp_row = {key: i * 128 for i, key in enumerate(wsp_keys)}
    QW_ROW = len(wsp_keys) * 128
    dram("wsp", [(len(wsp_keys) + 4) * 128, 2048], BF16, kind="Internal")

    sb = {}

    def sball(name, shape, dtype=F32):
        sb[name] = nc.alloc_sbuf_tensor("s_" + name, shape, dtype)

    sball("xchunk", [128, 2 * NST * 1024])
    sball("lnt", [128, NST * 1024])
    sball("lnf", [128, FC * CH])
    sball("kf", [128, H * CH])
    sball("ktm", [128, H * NST * 128])
    sball("vf", [128, NST * 1024])
    sball("qf", [128, H * CH])
    sball("gfb", [128, H * CH])
    sball("ysb", [128, H * CH])
    sball("attm", [128, NST * (NST + 1) * 64])
    sball("attm2", [128, NST * (NST + 1) * 64])
    sball("gyf", [128, H * CH])
    sball("yr", [128, NST * 1024])
    sball("ln2f_h", [128, FC * CH], BF16)
    sball("ln2f_l", [128, FC * CH], BF16)
    sball("hf_h", [128, HC * CH], BF16)
    sball("hf_l", [128, HC * CH], BF16)
    sball("stats", [128, 2 * NT])
    sball("st2", [128, 2 * NST])
    sball("sraw", [128, 8])
    sball("small", [128, 8])
    sball("sst", [128, H * DH])
    sball("wpool", [128, 4 * 1024])
    sball("rope2", [128, 2 * 2 * NST * 64])
    sball("gq", [128, NST * 1024])
    sball("gka", [128, NST * 1024])
    sball("gsf", [128, NST * 1024])
    sball("tri", [128, 128])
    sball("ident", [128, 128])
    sball("identb", [128, 128], BF16)
    sball("ones1", [1, 128])
    sball("sel8", [8, H * 128])
    sball("head_sel", [128, H * 8])
    sball("lnsb", [128, 2 * FC])
    sball("gnsb", [128, depth * 2 * FC])
    sball("fsc", [128, 1024])
    sball("bias2", [128, D])
    sball("qwt", [128, FC * 1024], BF16)
    sball("qxf", [128, 2 * FC * 128], BF16)
    sball("qbf", [128, 1024], BF16)
    sball("maxc", [128, 1])
    sball("gmaxc", [128, 4])
    sball("wsc", [128, 1])
    sball("thc", [128, 2])
    sball("qsc", [128, 1])
    sball("sic", [128, 1])
    sball("srow", [1, 4])
    sball("epsc", [128, 1])
    sball("cmask", [128, 2])
    sb["qtm"] = sb["ysb"].ap()
    sb["ropescr"] = sb["hf_h"].ap().bitcast(F32)[:, 0:2048]
    sb["gstat"] = sb["hf_l"].ap().bitcast(F32)[0:8, 0:1024]
    sb["lnf_h"] = sb["qwt"].ap()[:, 0:2048]
    sb["lnf_l"] = sb["qwt"].ap()[:, 2048:4096]
    sb["gyf_h"] = sb["qwt"].ap()[:, 4096:6144]
    sb["gyf_l"] = sb["qwt"].ap()[:, 6144:8192]
    wpb = sb["wpool"].ap().bitcast(BF16)

    pb = [nc.alloc_psum_tensor(f"pb{i}", [128, 512], F32) for i in range(7)]
    pbq = nc.alloc_psum_tensor("pbq", [128, 1024], BF16)
    PBQ = "q"

    cst = []
    for name, dst in (("gq_t", "gq"), ("gka_t", "gka"), ("gsf_t", "gsf"),
                      ("tri", "tri"), ("ident", "ident"), ("ones1", "ones1"),
                      ("sel8", "sel8"), ("head_sel", "head_sel"),
                      ("ln_sb", "lnsb"), ("gn_sb", "gnsb"),
                      ("fbias", "bias2"), ("cmask", "cmask")):
        cst.append(b.emit("sync", lambda e, d=sb[dst], s=dr[name]:
                          e.dma_start(out=d[:, :], in_=s.ap()),
                          inc="cst", by=16))
    CST = cst[-1]
    t_bias = b.emit("vector", lambda e: e.tensor_scalar_mul(
        sb["bias2"][:, :], sb["bias2"][:, :], 0.5), deps=[CST], inc="ve")
    t_idb = b.emit("vector", lambda e: e.tensor_copy(
        sb["identb"][:, :], sb["ident"][:, :]), deps=[CST], inc="ve")
    t_eps = b.emit("vector", lambda e: e.memset(sb["epsc"][:, :], EPS),
                   inc="ve")

    wst = {"n": 0, "free": [None] * 4}

    def wload(src_ap):
        slot = wst["n"] % 4
        wst["n"] += 1
        dstt = sb["wpool"][:, slot * 1024:(slot + 1) * 1024]
        tok = b.emit("sync", lambda e, d=dstt, s=src_ap:
                     e.dma_start(out=d, in_=s),
                     deps=[wst["free"][slot]], inc=f"wld{slot}", by=16)
        return slot, dstt, tok

    def wfree(slot, tok):
        wst["free"][slot] = tok

    wsp_tok = {}

    def wload_sp2(key_a, key_b):
        ss = (wst["n"] // 2) % 2
        wst["n"] += 2
        dstt = wpb[:, ss * 4096:(ss + 1) * 4096]
        r0 = wsp_row[key_a]
        assert wsp_row[key_b] == r0 + 128, (key_a, key_b)
        src = dr["wsp"].ap()[r0:r0 + 256, :].rearrange(
            "(two p) c -> p two c", p=128)
        dst3 = dstt.rearrange("p (two c) -> p two c", two=2)
        dd = [wst["free"][2 * ss], wst["free"][2 * ss + 1],
              wsp_tok[key_a], wsp_tok[key_b]]
        tok = None
        for hv in range(2):
            tok = b.emit("sync", lambda e,
                         d=dst3[:, :, hv * 1024:(hv + 1) * 1024],
                         s=src[:, :, hv * 1024:(hv + 1) * 1024]:
                         e.dma_start(out=d, in_=s),
                         deps=dd, inc=f"wlp{ss}", by=16)
        return ss, dstt, tok

    def wfree2(ss, tok):
        wst["free"][2 * ss] = tok
        wst["free"][2 * ss + 1] = tok

    def emit_wsplit_gen():
        kstg = sb["kf"].ap().bitcast(BF16)
        last = {0: None, 1: None}
        for i, key in enumerate(wsp_keys):
            nm, k_, hf_ = key
            if hf_ is None:
                src = dr[nm][k_ * 128:(k_ + 1) * 128, :]
            else:
                src = dr[nm][k_ * 128:(k_ + 1) * 128,
                             hf_ * 1024:(hf_ + 1) * 1024]
            slot, wt, wtok = wload(src)
            par = i % 2
            stg = kstg[:, par * 2048:(par + 1) * 2048]
            hi = b.emit("scalar", lambda e, o=stg[:, 0:1024], i_=wt:
                        e.copy(o, i_), deps=[wtok, last[par]], inc="sc")
            lo = b.emit("vector", lambda e, o=stg[:, 1024:2048], i_=wt,
                        h_=stg[:, 0:1024]: e.tensor_sub(o, i_, h_),
                        deps=[hi], inc="ve")
            wfree(slot, lo)
            r0 = wsp_row[key]
            gw = f"gws{par}"
            stok = b.emit("gpsimd", lambda e, d=dr["wsp"][r0:r0 + 128, :],
                          s_=stg: e.dma_start(out=d, in_=s_),
                          deps=[lo, (gw, b.cnt.get(gw, 0))], inc=gw, by=16)
            wsp_tok[key] = stok
            last[par] = stok
            yield
        for h in range(H):
            b.note(("kfh", h), last[h // 4])

    def emit_qw_reload():
        toks = []
        for jj in range(2):
            r0 = QW_ROW + jj * 256
            src = dr["wsp"].ap()[r0:r0 + 256, :].rearrange(
                "(two p) c -> p two c", p=128)
            dst3 = sb["qwt"].ap()[:, jj * 4096:(jj + 1) * 4096].rearrange(
                "p (two c) -> p two c", two=2)
            toks.append(b.emit("sync", lambda e, d=dst3, s=src:
                        e.dma_start(out=d, in_=s),
                        deps=[("qst", b.cnt.get("qst", 0)),
                              ("te", b.cnt.get("te", 0)),
                              b.last("qwtv"), b.last("qwt")],
                        inc=f"qrl{jj}", by=16))
        b.note("qwt", toks)
        return toks

    def stats_pass(x_ap, mu_ap, r_ap, deps, skey):
        s0 = sb["sraw"][:, 0:1]
        s1 = sb["sraw"][:, 1:2]
        t0 = b.emit("vector", lambda e: e.reduce_sum(s0, x_ap, AX.X),
                    deps=deps + [b.last("s0")], inc="ve")
        t1 = b.emit("scalar", lambda e: e.activation(
            sb["fsc"][:, :], x_ap, AF.Square, accum_out=s1),
            deps=deps + [b.last("s1"), b.last("fsc")], inc="sc")
        b.note("fsc", t1)
        t2 = b.emit("vector", lambda e: e.tensor_scalar_mul(mu_ap, s0, 1.0 / D),
                    deps=[t0, b.last(("str", skey))], inc="ve")
        b.note("s0", t2)
        t3 = b.emit("vector", lambda e: e.tensor_scalar_mul(
            sb["small"][:, 0:1], s1, 1.0 / D),
            deps=[t1, b.last("sm0")], inc="ve")
        b.note("s1", t3)
        t4 = b.emit("vector", lambda e: e.tensor_tensor(
            sb["small"][:, 1:2], mu_ap, mu_ap, ALU.mult),
            deps=[t2, b.last("sm1")], inc="ve")
        t5 = b.emit("vector", lambda e: e.tensor_sub(
            sb["small"][:, 0:1], sb["small"][:, 0:1], sb["small"][:, 1:2]),
            deps=[t3, t4], inc="ve")
        b.note("sm1", t5)
        t6 = b.emit("scalar", lambda e: e.activation(
            sb["small"][:, 2:3], sb["small"][:, 0:1], AF.Sqrt,
            bias=sb["epsc"][:, 0:1]),
            deps=[t5, t_eps, b.last("sm2")], inc="sc")
        b.note("sm0", t6)
        t7 = b.emit("vector", lambda e: e.reciprocal(r_ap, sb["small"][:, 2:3]),
                    deps=[t6, b.last(("str", skey))], inc="ve")
        b.note("sm2", t7)
        b.note(("str", skey), None)
        b.note(("stw", skey), t7)
        return t7

    def ln_fm(src_sb, src_off, mus, dname, deps_in):
        dh, dl = sb[dname + "_h"], sb[dname + "_l"]
        ln_toks = []
        for st in range(NST):
            xa = src_sb[:, src_off + st * 1024:src_off + (st + 1) * 1024]
            (mu, rr, rdy, skey) = mus[st]
            tk = b.emit("vector", lambda e,
                        o=sb["lnt"][:, st * 1024:(st + 1) * 1024],
                        i=xa, m=mu, r=rr:
                        e.tensor_scalar(o, i, m, r, ALU.subtract, ALU.mult),
                        deps=deps_in + [rdy, b.last(("lnt", st))], inc="ve")
            b.note_add(("str", skey), tk)
            ln_toks.append(tk)
        out_toks = []
        for fc in range(FC):
            bk = fc % 2
            tt = None
            for st in range(NST):
                ia = sb["lnt"][:, st * 1024 + fc * 128:st * 1024 + (fc + 1) * 128]
                dd = [ln_toks[st], CST] + (b.bw(bk) if st == 0 else [])
                tt = b.emit("tensor", lambda e,
                            o=pb[bk][:, st * 128:(st + 1) * 128], i=ia:
                            e.transpose(o, i, sb["ident"][:, :]),
                            deps=dd, inc=("te" if st == NST - 1 else None))
            b.set_w(bk, tt)
            fscr = sb["lnf"][:, fc * CH:(fc + 1) * CH]
            sc_ = sb["lnsb"][:, fc:fc + 1]
            bc_ = sb["lnsb"][:, FC + fc:FC + fc + 1]
            t32 = b.emit("vector", lambda e, o=fscr, i=pb[bk][:, 0:CH],
                         s=sc_, bb=bc_:
                         e.tensor_scalar(o, i, s, bb, ALU.mult, ALU.add),
                         deps=[tt, b.last(("fm32", fc))], inc="ve")
            b.add_r(bk, t32)
            b.note(("fm32", fc), t32)
            qd = [b.last("qwt")] if dname == "lnf" else []
            thi = b.emit("scalar", lambda e, o=dh[:, fc * CH:(fc + 1) * CH],
                         i=fscr: e.copy(o, i),
                         deps=[t32, b.last(("fmh", dname, fc))] + qd,
                         inc="sc")
            tlo = b.emit("vector", lambda e, o=dl[:, fc * CH:(fc + 1) * CH],
                         i=fscr, h_=dh[:, fc * CH:(fc + 1) * CH]:
                         e.tensor_sub(o, i, h_),
                         deps=[thi, b.last(("fml", dname, fc))], inc="ve")
            b.note(("fmh", dname, fc), [thi, tlo])
            b.note(("fml", dname, fc), tlo)
            if dname == "lnf":
                b.note_add("qwtv", [thi, tlo])
            out_toks.append([thi, tlo])
        for st in range(NST):
            b.note(("lnt", st), [p for pr in out_toks for p in pr])
        return out_toks

    def emit_norm(src_sb, src_off, mus, deps_in):
        ln_toks = []
        for st in range(NST):
            xa = src_sb[:, src_off + st * 1024:src_off + (st + 1) * 1024]
            (mu, rr, rdy, skey) = mus[st]
            tk = b.emit("vector", lambda e,
                        o=sb["lnt"][:, st * 1024:(st + 1) * 1024],
                        i=xa, m=mu, r=rr:
                        e.tensor_scalar(o, i, m, r, ALU.subtract, ALU.mult),
                        deps=deps_in + [rdy, b.last(("lnt", st))], inc="ve")
            b.note_add(("str", skey), tk)
            ln_toks.append(tk)
        return ln_toks

    def ln_proj_fused(src_sb, src_off, mus, dname, deps_in, wkeys, mm_fc,
                      pre_ln=None):
        dh, dl = sb[dname + "_h"], sb[dname + "_l"]
        if pre_ln is not None:
            ln_toks = pre_ln
        else:
            ln_toks = emit_norm(src_sb, src_off, mus, deps_in)

        def transp(fc):
            bk = 4 + (fc % 2)
            tt = None
            for st in range(NST):
                ia = sb["lnt"][:, st * 1024 + fc * 128:
                               st * 1024 + (fc + 1) * 128]
                dd = [ln_toks[st], CST] + (b.bw(bk) if st == 0 else [])
                tt = b.emit("tensor", lambda e,
                            o=pb[bk][:, st * 128:(st + 1) * 128], i=ia:
                            e.transpose(o, i, sb["ident"][:, :]),
                            deps=dd, inc=("te" if st == NST - 1 else None))
            b.set_w(bk, tt)
            return tt

        out_toks = []
        tt_fc = transp(0)
        wctx = {}
        for fc in range(FC):
            bk = 4 + (fc % 2)
            fscr = sb["lnf"][:, fc * CH:(fc + 1) * CH]
            sc_ = sb["lnsb"][:, fc:fc + 1]
            bc_ = sb["lnsb"][:, FC + fc:FC + fc + 1]
            t32 = b.emit("vector", lambda e, o=fscr, i=pb[bk][:, 0:CH],
                         s=sc_, bb=bc_:
                         e.tensor_scalar(o, i, s, bb, ALU.mult, ALU.add),
                         deps=[tt_fc, b.last(("fm32", fc))], inc="ve")
            b.add_r(bk, t32)
            b.note(("fm32", fc), t32)
            qd = [b.last("qwt")] if dname == "lnf" else []
            thi = b.emit("scalar", lambda e, o=dh[:, fc * CH:(fc + 1) * CH],
                         i=fscr: e.copy(o, i),
                         deps=[t32, b.last(("fmh", dname, fc))] + qd,
                         inc="sc")
            tlo = b.emit("vector", lambda e, o=dl[:, fc * CH:(fc + 1) * CH],
                         i=fscr, h_=dh[:, fc * CH:(fc + 1) * CH]:
                         e.tensor_sub(o, i, h_),
                         deps=[thi, b.last(("fml", dname, fc))], inc="ve")
            b.note(("fmh", dname, fc), [thi, tlo])
            b.note(("fml", dname, fc), tlo)
            if dname == "lnf":
                b.note_add("qwtv", [thi, tlo])
            out_toks.append([thi, tlo])
            if fc % 2 == 0:
                wctx["cur"] = wload_sp2(*wkeys[fc // 2])
            if fc + 1 < FC:
                tt_fc = transp(fc + 1)
            ss_, wt2, wtok = wctx["cur"]
            wt = wt2[:, (fc % 2) * 2048:(fc % 2 + 1) * 2048]
            lastmm = mm_fc(fc, tlo, wt, wtok)
            if fc % 2 == 1:
                wfree2(ss_, lastmm)
        for st in range(NST):
            b.note(("lnt", st), [p for pr in out_toks for p in pr])
        return out_toks

    def mm_tm(dh, dl, vmm):
        def mm_fc(fc, tlo, wt, wtok):
            for p, (xs, ws) in enumerate(((0, 0), (0, 1), (1, 0))):
                lsrc = dh if xs == 0 else dl
                for j in range(NST * 2):
                    st, half = divmod(j, 2)
                    dd = [wtok, tlo]
                    if fc == 0 and p == 0:
                        dd += b.bw(j)
                    vmm[j] = b.emit(
                        "tensor", lambda e, p_=pb[j],
                        l=lsrc[:, fc * CH + st * 128:fc * CH + (st + 1) * 128],
                        r=wt[:, ws * 1024 + half * 512:
                             ws * 1024 + (half + 1) * 512],
                        s=(fc == 0 and p == 0),
                        sp=(fc == FC - 1 and p == 2):
                        e.matmul(p_[:, 0:512], l, r, start=s, stop=sp),
                        deps=dd,
                        inc=("te" if ((fc == FC - 1 and p == 2)
                                      or (p == 2 and j == NST * 2 - 1))
                             else None))
            return vmm[-1]
        return mm_fc

    def proj_evac(out_sb, vmm, okey, extra=()):
        proj_toks = []
        for j in range(NST * 2):
            b.set_w(j, vmm[j])
            da = out_sb[:, j * 512:(j + 1) * 512]
            tk = b.emit("scalar", lambda e, o=da, i=pb[j][:, 0:512]:
                        e.copy(o, i),
                        deps=[vmm[j], b.last((okey, j))] + list(extra),
                        inc="sc")
            b.add_r(j, tk)
            b.note((okey, j), tk)
            proj_toks.append(tk)
        return proj_toks

    def proj_fm(wname, in_sb, in_tok, out_sb, silu=False):
        mm_bank = [None] * 4
        for kk in range(0, FC, 2):
          ss_, wt2, wtok = wload_sp2((wname, kk, None), (wname, kk + 1, None))
          for dk in range(2):
            k = kk + dk
            wt = wt2[:, dk * 2048:(dk + 1) * 2048]
            last_of_k = None
            if True:
              for p, (xs, ws) in enumerate(((0, 0), (0, 1), (1, 0))):
                rsrc = sb["lnf_h"] if xs == 0 else sb["lnf_l"]
                for h in range(H):
                    bk = h // 2
                    dd = [wtok, in_tok]
                    if k == 0 and p == 0 and h % 2 == 0:
                        dd += b.bw(bk)
                    want_tok = (k == FC - 1 and p == 2) and (h % 2 == 1)
                    want_slot = (p == 2 and h == H - 1)
                    mm = b.emit("tensor", lambda e,
                                p_=pb[bk][:, (h % 2) * CH:(h % 2 + 1) * CH],
                                l=wt[:, ws * 1024 + h * 128:
                                     ws * 1024 + (h + 1) * 128],
                                r=rsrc[:, k * CH:(k + 1) * CH],
                                s=(k == 0 and p == 0 and h % 2 == 0),
                                sp=(k == FC - 1 and p == 2 and h % 2 == 1):
                                e.matmul(p_, l, r, start=s, stop=sp),
                                deps=dd,
                                inc=("te" if (want_tok or want_slot)
                                     else None))
                    if want_tok:
                        mm_bank[bk] = mm
                    if want_slot:
                        last_of_k = mm
          wfree2(ss_, last_of_k)
        out_toks = []
        for bk in range(4):
            b.set_w(bk, mm_bank[bk])
            da = out_sb[:, bk * 2 * CH:(bk + 1) * 2 * CH]
            if silu:
                tk = b.emit("scalar", lambda e, o=da, i=pb[bk][:, 0:512]:
                            e.activation(o, i, AF.Silu),
                            deps=[mm_bank[bk], b.last(("pf", id(out_sb), bk))],
                            inc="sc")
            else:
                tk = b.emit("scalar", lambda e, o=da, i=pb[bk][:, 0:512]:
                            e.copy(o, i),
                            deps=[mm_bank[bk], b.last(("pf", id(out_sb), bk))],
                            inc="sc")
            b.add_r(bk, tk)
            b.note(("pf", id(out_sb), bk), tk)
            out_toks.append(tk)
        return out_toks

    def proj_tm(wname, out_sb, in_tok, okey, extra=()):
        vmm = [None] * (NST * 2)
        for kk in range(0, FC, 2):
          ss_, wt2, wtok = wload_sp2((wname, kk, None), (wname, kk + 1, None))
          for dk in range(2):
            k = kk + dk
            wt = wt2[:, dk * 2048:(dk + 1) * 2048]
            if True:
              for p, (xs, ws) in enumerate(((0, 0), (0, 1), (1, 0))):
                lsrc = sb["lnf_h"] if xs == 0 else sb["lnf_l"]
                for j in range(NST * 2):
                    st, half = divmod(j, 2)
                    bk = j
                    dd = [wtok, in_tok]
                    if k == 0 and p == 0:
                        dd += b.bw(bk)
                    vmm[j] = b.emit(
                        "tensor", lambda e, p_=pb[bk],
                        l=lsrc[:, k * CH + st * 128:k * CH + (st + 1) * 128],
                        r=wt[:, ws * 1024 + half * 512:
                             ws * 1024 + (half + 1) * 512],
                        s=(k == 0 and p == 0),
                        sp=(k == FC - 1 and p == 2):
                        e.matmul(p_[:, 0:512], l, r, start=s, stop=sp),
                        deps=dd,
                        inc=("te" if ((k == FC - 1 and p == 2)
                                      or (p == 2 and j == NST * 2 - 1))
                             else None))
          wfree2(ss_, vmm[-1])
        out_toks = []
        for j in range(NST * 2):
            bk = j
            b.set_w(bk, vmm[j])
            da = out_sb[:, j * 512:(j + 1) * 512]
            tk = b.emit("scalar", lambda e, o=da, i=pb[bk][:, 0:512]:
                        e.copy(o, i),
                        deps=[vmm[j], b.last((okey, j))] + list(extra),
                        inc="sc")
            b.add_r(bk, tk)
            b.note((okey, j), tk)
            out_toks.append(tk)
        return out_toks

    def rope_tm(buf, rpar, rtoks, in_toks, sign):
        scr = sb["ropescr"]
        done = []
        for st in range(NST):
            bufst = buf[:, st * 1024:(st + 1) * 1024]
            x1 = bufst.rearrange("p (h b) -> p h b", h=H)[:, :, 0:64]
            x2 = bufst.rearrange("p (h b) -> p h b", h=H)[:, :, 64:128]
            co = sb["rope2"][:, rpar * 256 + st * 64:
                             rpar * 256 + (st + 1) * 64]
            si = sb["rope2"][:, rpar * 256 + 128 + st * 64:
                             rpar * 256 + 128 + (st + 1) * 64]
            cb = co[:, None, :].broadcast_to([128, H, 64])
            sib = si[:, None, :].broadcast_to([128, H, 64])
            s1 = scr[:, 0:512].rearrange("p (h b) -> p h b", h=H)
            s2 = scr[:, 512:1024].rearrange("p (h b) -> p h b", h=H)
            s3 = scr[:, 1024:1536].rearrange("p (h b) -> p h b", h=H)
            s4 = scr[:, 1536:2048].rearrange("p (h b) -> p h b", h=H)
            dd = [in_toks, rtoks, b.last("ropescr")] + \
                [b.last(("hf", i)) for i in range(8)]
            t1 = b.emit("vector", lambda e, o=s1, a=x1, bb=cb:
                        e.tensor_tensor(o, a, bb, ALU.mult), deps=dd, inc="ve")
            t2 = b.emit("vector", lambda e, o=s2, a=x2, bb=sib:
                        e.tensor_tensor(o, a, bb, ALU.mult), deps=dd, inc="ve")
            t3 = b.emit("vector", lambda e, o=s3, a=x1, bb=sib:
                        e.tensor_tensor(o, a, bb, ALU.mult), deps=dd, inc="ve")
            t4 = b.emit("vector", lambda e, o=s4, a=x2, bb=cb:
                        e.tensor_tensor(o, a, bb, ALU.mult), deps=dd, inc="ve")
            op1 = ALU.subtract if sign > 0 else ALU.add
            t5 = b.emit("vector", lambda e, o=x1, a=s1, bb=s2, op=op1:
                        e.tensor_tensor(o, a, bb, op),
                        deps=[t1, t2, t3, t4], inc="ve")
            if sign > 0:
                t6 = b.emit("vector", lambda e, o=x2, a=s3, bb=s4:
                            e.tensor_tensor(o, a, bb, ALU.add),
                            deps=[t2, t3, t4], inc="ve")
            else:
                t6 = b.emit("vector", lambda e, o=x2, a=s4, bb=s3:
                            e.tensor_tensor(o, a, bb, ALU.subtract),
                            deps=[t2, t3, t4], inc="ve")
            b.note("ropescr", [t5, t6])
            for i in range(8):
                b.note_add(("hf", i), [t5, t6])
            done.append([t5, t6])
        return done

    def emit_qw_prep():
        tz = b.emit("vector", lambda e: e.memset(sb["maxc"][:, :], 0.0),
                    deps=[CST], inc="ve")
        prev = tz
        for o in range(FC):
            slot, wt, wtok = wload(dr["fw"][o * 128:(o + 1) * 128, :])
            t1 = b.emit("vector", lambda e, i=wt: e.reduce_sum(
                sb["small"][:, 3:4], i, AX.X, apply_absolute_value=True),
                deps=[wtok, b.last("sm3")], inc="ve")
            prev = b.emit("vector", lambda e: e.tensor_add(
                sb["maxc"][:, :], sb["maxc"][:, :], sb["small"][:, 3:4]),
                deps=[t1, prev], inc="ve")
            b.note("sm3", prev)
            wfree(slot, t1)
        dd = b.bw(6)
        tT = b.emit("tensor", lambda e: e.transpose(
            pb[6][0:1, 0:128], sb["maxc"][:, :], sb["ident"][:, :]),
            deps=[prev, CST] + dd, inc="te")
        b.set_w(6, tT)
        b.note("maxc", tT)
        tR = b.emit("vector", lambda e: e.reduce_sum(
            sb["srow"][0:1, 0:1], pb[6][0:1, 0:128], AX.X),
            deps=[tT, b.last("srow")], inc="ve")
        b.add_r(6, tR)
        tS = b.emit("vector", lambda e: e.tensor_scalar_mul(
            sb["srow"][0:1, 0:1], sb["srow"][0:1, 0:1], 1.0 / (D * D)),
            deps=[tR], inc="ve")
        dd = b.bw(6)
        tB = b.emit("tensor", lambda e: e.matmul(
            pb[6][:, 0:1], sb["ones1"][0:1, :], sb["srow"][0:1, 0:1],
            start=True, stop=True), deps=[tS, CST] + dd, inc="te")
        b.set_w(6, tB)
        b.note("srow", tB)
        tW = b.emit("vector", lambda e: e.tensor_copy(
            sb["wsc"][:, :], pb[6][:, 0:1]), deps=[tB], inc="ve")
        b.add_r(6, tW)
        tTh = b.emit("vector", lambda e: e.tensor_scalar_mul(
            sb["thc"][:, 0:1], sb["wsc"][:, :], 0.5), deps=[tW], inc="ve")
        tTh2 = b.emit("vector", lambda e: e.tensor_scalar_mul(
            sb["thc"][:, 1:2], sb["thc"][:, 0:1], -1.0), deps=[tTh], inc="ve")
        qv = None
        for o in range(FC):
            slot, wt, wtok = wload(dr["fw"][o * 128:(o + 1) * 128, :])
            tg = b.emit("vector", lambda e, i=wt: e.tensor_scalar(
                sb["fsc"][:, :], i, sb["thc"][:, 0:1], None, ALU.is_gt),
                deps=[wtok, tTh, b.last("fsc")], inc="ve")
            tl = b.emit("vector", lambda e, i=wt: e.tensor_scalar(
                sb["lnt"][:, 0:1024], i, sb["thc"][:, 1:2], None, ALU.is_lt),
                deps=[wtok, tTh2, b.last(("lnt", 0))], inc="ve")
            tq = b.emit("vector", lambda e: e.tensor_sub(
                sb["qxf"][:, 0:1024], sb["fsc"][:, 0:1024],
                sb["lnt"][:, 0:1024]),
                deps=[tg, tl, b.last(("qxf", 0))], inc="ve")
            b.note(("lnt", 0), tq)
            b.note("fsc", tq)
            wfree(slot, tq)
            tt = None
            for kc in range(FC):
                dd = [tq, t_idb] + (b.bw(PBQ) if kc == 0 else [])
                tt = b.emit("tensor", lambda e,
                            o_=pbq[:, kc * 128:(kc + 1) * 128],
                            i=sb["qxf"][:, kc * 128:(kc + 1) * 128]:
                            e.transpose(o_, i, sb["identb"][:, :]),
                            deps=dd, inc=("te" if kc == FC - 1 else None))
            b.set_w(PBQ, tt)
            b.note(("qxf", 0), tt)
            qv = b.emit("vector", lambda e, o_=o: e.tensor_copy(
                sb["qwt"].ap().rearrange("p (k o) -> p k o", k=FC)
                [:, :, o_ * 128:(o_ + 1) * 128],
                pbq[:, :].rearrange("p (k o) -> p k o", o=128)),
                deps=[tt, b.last("qwtv"), b.last("qwt")], inc="ve")
            b.add_r(PBQ, qv)
            b.note_add("qwt", qv)
        for sub in range(4):
            r0 = QW_ROW + sub * 128
            b.emit("sync", lambda e, d=dr["wsp"][r0:r0 + 128, :],
                   s_=sb["qwt"].ap()[:, sub * 2048:(sub + 1) * 2048]:
                   e.dma_start(out=d, in_=s_),
                   deps=[qv], inc="qst", by=16)
        return qv, tTh

    def emit_ffn_outer(src_dram, dst_dram, qw_tok, ws_tok, rd_marks,
                       war_marks, bg=None, qw_loader=None):
        def pump(n):
            if bg is not None:
                for _ in range(n):
                    if next(bg, "END") == "END":
                        return

        tz = b.emit("vector", lambda e: e.memset(sb["maxc"][:, :], 0.0),
                    deps=[b.last("maxc")], inc="ve")
        prevmax = tz
        for tt in range(NT):
            par = tt % 4
            xa = sb["xchunk"][:, par * 1024:(par + 1) * 1024]
            xtok = b.emit("sync", lambda e, d=xa,
                          s=dr[src_dram][tt * 128:(tt + 1) * 128, :]:
                          e.dma_start(out=d, in_=s),
                          deps=[b.last(("xc", par))] + rd_marks(tt),
                          inc=f"ldx{par}", by=16)
            mu = sb["stats"][:, tt:tt + 1]
            rr = sb["stats"][:, NT + tt:NT + tt + 1]
            ts_ = stats_pass(xa, mu, rr, [xtok], tt)
            tln = b.emit("vector", lambda e, i=xa, m=mu, r=rr: e.tensor_scalar(
                sb["lnt"][:, 0:1024], i, m, r, ALU.subtract, ALU.mult),
                deps=[ts_, xtok, b.last(("lnt", 0))], inc="ve")
            b.note(("xc", par), tln)
            b.note_add(("str", tt), tln)
            tmx = b.emit("vector", lambda e: e.reduce_max(
                sb["small"][:, 4:5], sb["lnt"][:, 0:1024], AX.X,
                apply_absolute_value=True),
                deps=[tln, b.last("sm4")], inc="ve")
            b.note(("lnt", 0), tmx)
            prevmax = b.emit("vector", lambda e: e.tensor_max(
                sb["maxc"][:, :], sb["maxc"][:, :], sb["small"][:, 4:5]),
                deps=[tmx, prevmax], inc="ve")
            b.note("sm4", prevmax)
            pump(9)
        if qw_loader is not None:
            qw_tok = qw_loader()
        dd = b.bw(6)
        tT = b.emit("tensor", lambda e: e.transpose(
            pb[6][0:1, 0:128], sb["maxc"][:, :], sb["ident"][:, :]),
            deps=[prevmax, CST] + dd, inc="te")
        b.set_w(6, tT)
        b.note("maxc", tT)
        tR = b.emit("vector", lambda e: e.reduce_max(
            sb["srow"][0:1, 0:1], pb[6][0:1, 0:128], AX.X),
            deps=[tT, b.last("srow")], inc="ve")
        b.add_r(6, tR)
        dd = b.bw(6)
        tB = b.emit("tensor", lambda e: e.matmul(
            pb[6][:, 0:4], sb["ones1"][0:1, :],
            sb["srow"][0:1, 0:1].broadcast_to([1, 4]), start=True, stop=True),
            deps=[tR, CST] + dd, inc="te")
        b.set_w(6, tB)
        b.note("srow", tB)
        tC = b.emit("vector", lambda e: e.tensor_copy(
            sb["gmaxc"][:, :], pb[6][:, 0:4]),
            deps=[tB, b.last("gmaxc")], inc="ve")
        b.add_r(6, tC)
        if collective:
            tSd = b.emit("gpsimd", lambda e: e.dma_start(
                out=dr["bmax_i"][:, :], in_=sb["gmaxc"][:, :]),
                deps=[tC, b.last("gbc")], inc="gbc", by=16)
            tCC = b.emit("gpsimd", lambda e: e.collective_compute(
                "AllReduce", ALU.max, replica_groups=[list(range(n_cores))],
                ins=[dr["bmax_i"].ap().opt()], outs=[dr["bmax_o"].ap().opt()]),
                deps=[tSd], inc="cc")
            tL = b.emit("gpsimd", lambda e: e.dma_start(
                out=sb["gmaxc"][:, :], in_=dr["bmax_o"][:, :]),
                deps=[tCC], inc="gbc", by=16)
            b.note("gbc", tL)
        else:
            tL = tC
        b.note("gmaxc", tL)
        t1 = b.emit("vector", lambda e: e.reciprocal(
            sb["qsc"][:, :], sb["gmaxc"][:, 0:1]), deps=[tL], inc="ve")
        qsc_tok = b.emit("vector", lambda e: e.tensor_scalar_mul(
            sb["qsc"][:, :], sb["qsc"][:, :], 127.0), deps=[t1], inc="ve")
        t3 = b.emit("vector", lambda e: e.tensor_tensor(
            sb["sic"][:, :], sb["gmaxc"][:, 0:1], sb["wsc"][:, :], ALU.mult),
            deps=[tL, ws_tok], inc="ve")
        sic_tok = b.emit("vector", lambda e: e.tensor_scalar_mul(
            sb["sic"][:, :], sb["sic"][:, :], 0.5 / 127.0), deps=[t3],
            inc="ve")
        out_marks = {}
        gfbb = sb["gfb"].ap().bitcast(BF16)

        def q_stage(tt):
            par = tt % 2
            par4 = tt % 4
            xa = sb["xchunk"][:, par4 * 1024:(par4 + 1) * 1024]
            xdep = [b.last(("xc", par4))] + rd_marks(tt)
            for hv in range(2):
                xtok = b.emit("sync", lambda e,
                              d=xa[:, hv * 512:(hv + 1) * 512],
                              s=dr[src_dram][tt * 128:(tt + 1) * 128,
                                             hv * 512:(hv + 1) * 512]:
                              e.dma_start(out=d, in_=s),
                              deps=xdep, inc=f"ldx{par4}", by=16)
            mu = sb["stats"][:, tt:tt + 1]
            rr = sb["stats"][:, NT + tt:NT + tt + 1]
            sm = sb["small"][:, 5 + par:6 + par]
            trq = b.emit("vector", lambda e, r=rr, o=sm: e.tensor_tensor(
                o, r, sb["qsc"][:, :], ALU.mult),
                deps=[qsc_tok, b.last(("sm5", par)), b.last(("stw", tt))],
                inc="ve")
            lnq = sb["lnt"][:, par * 1024:(par + 1) * 1024]
            tqa = b.emit("vector", lambda e, i=xa, m=mu, o=lnq, s_=sm:
                         e.tensor_scalar(o, i, m, s_, ALU.subtract, ALU.mult),
                         deps=[xtok, trq, b.last(("lnt", par))], inc="ve")
            b.note(("sm5", par), tqa)
            b.note_add(("str", tt), tqa)
            sgn = gfbb[:, par * 1024:(par + 1) * 1024]
            gkey = ("pf", id(sb["gfb"]), par)
            tsg = b.emit("scalar", lambda e, o=sgn, i=lnq: e.activation(
                o, i, AF.Sign), deps=[tqa, b.last(gkey)], inc="sc")
            ab = sb["vf"][:, par * 1024:(par + 1) * 1024]
            vkeys = [("vf", 2 * par), ("vf", 2 * par + 1)]
            tabs = b.emit("scalar", lambda e, o=ab, i=lnq: e.activation(
                o, i, AF.Abs),
                deps=[tqa, b.last(vkeys[0]), b.last(vkeys[1])], inc="sc")
            trc = b.emit("scalar", lambda e, o=sb["qbf"][:, :], i=ab:
                         e.activation(o, i, AF.Copy, bias=128.0),
                         deps=[tabs, b.last("qbf")], inc="sc")
            b.note(vkeys[0], trc)
            b.note(vkeys[1], trc)
            qx = sb["qxf"][:, par * 1024:(par + 1) * 1024]
            tr3 = b.emit("vector", lambda e, o=qx, s_=sgn:
                         e.scalar_tensor_tensor(o, sb["qbf"][:, :], 128.0,
                                                s_, ALU.subtract, ALU.mult),
                         deps=[trc, tsg, b.last(("qxf", par))], inc="ve")
            b.note("qbf", tr3)
            b.note(gkey, tr3)
            b.note(("lnt", par), [tsg, tabs])
            tt_ = None
            for kc in range(FC):
                dd = [tr3, t_idb] + (b.bw(PBQ) if kc == 0 else [])
                tt_ = b.emit("tensor", lambda e,
                             o_=pbq[:, kc * 128:(kc + 1) * 128],
                             i=qx[:, kc * 128:(kc + 1) * 128]:
                             e.transpose(o_, i, sb["identb"][:, :]),
                             deps=dd, inc=("te" if kc == FC - 1 else None))
            b.set_w(PBQ, tt_)
            tqf = b.emit("scalar", lambda e, o=qx: e.copy(
                o, pbq[:, :]), deps=[tt_], inc="sc")
            b.add_r(PBQ, tqf)
            return (tt, xa, qx, tqf, xtok)

        def m_stage(ctxq):
            tt, xa, qx, tqf, xtok = ctxq
            par = tt % 2
            par4 = tt % 4
            mm_last = [None, None]
            for kc in range(FC):
                for half in range(2):
                    bk = half
                    dd = [tqf, qw_tok]
                    if kc == 0:
                        dd += b.bw(bk)
                    mm_last[half] = b.emit(
                        "tensor", lambda e, p_=pb[bk],
                        l=qx[:, kc * 128:(kc + 1) * 128],
                        r=sb["qwt"][:, kc * 1024 + half * 512:
                                    kc * 1024 + (half + 1) * 512],
                        s=(kc == 0), sp=(kc == FC - 1):
                        e.matmul(p_[:, 0:512], l, r, start=s, stop=sp),
                        deps=dd, inc=("te" if kc == FC - 1 else None))
            b.note(("qxf", par), [mm_last[0], mm_last[1]])
            ev = []
            for half in range(2):
                bk = half
                b.set_w(bk, mm_last[half])
                o1 = sb["fsc"][:, half * 512:(half + 1) * 512]
                e1 = b.emit("vector", lambda e, o=o1, i=pb[bk][:, 0:512],
                            x_=xa[:, half * 512:(half + 1) * 512]:
                            e.scalar_tensor_tensor(o, i, sb["sic"][:, 0:1],
                                                   x_, ALU.mult, ALU.add),
                            deps=[mm_last[half], sic_tok, xtok,
                                  b.last("fsc")], inc="ve")
                b.add_r(bk, e1)
                e2 = b.emit("vector", lambda e, o=o1,
                            bb=sb["bias2"][:, half * 512:(half + 1) * 512]:
                            e.tensor_add(o, o, bb), deps=[e1, t_bias],
                            inc="ve")
                ev.append(e2)
            b.note(("xc", par4), ev)
            gs_name = f"gst{tt % 4}"
            drain = [(gs_name, b.cnt.get(gs_name, 0))]
            stok = b.emit("gpsimd", lambda e,
                          d=dr[dst_dram][tt * 128:(tt + 1) * 128, :],
                          s=sb["fsc"][:, 0:1024]: e.dma_start(out=d, in_=s),
                          deps=ev + war_marks(tt) + drain, inc=gs_name, by=16)
            b.note("fsc", stok)
            out_marks[tt] = stok

        prevq = None
        for tt in range(NT):
            cur = q_stage(tt)
            if prevq is not None:
                m_stage(prevq)
            prevq = cur
        m_stage(prevq)
        return out_marks

    def emit_layer(li, xin, xout, rd_marks, war_marks, mode="merged",
                   ctx=None):
        out_marks = {}
        if mode != "B":
            tz = b.emit("vector", lambda e: e.memset(sb["sst"][:, :], 0.0),
                        deps=[b.last("sst")], inc="ve")
            b.note("sst", tz)
        if ctx is None:
            ctx = {}
        ctx.setdefault("mus", {})
        ctx.setdefault("kv", {})
        snap = sb["ktm"].ap()[:, 0:1024]
        sin_m = sb["ktm"].ap()[:, 1024:2048]
        xld = {}
        rld = {}

        def load_chunk(c):
            par = c % 2
            toks = []
            for st in range(NST):
                t0 = c * CH + st * 128
                slotid = par * NST + st
                da = sb["xchunk"][:, slotid * 1024:(slotid + 1) * 1024]
                toks.append(b.emit(
                    "sync", lambda e, d=da, s=dr[xin][t0:t0 + 128, :]:
                    e.dma_start(out=d, in_=s),
                    deps=[b.last(("xc", slotid))] + rd_marks(c * NST + st),
                    inc=f"ldx{slotid}", by=16))
            xld[c] = toks
            if mode != "B":
                mus = []
                for st in range(NST):
                    slotid = par * NST + st
                    xa = sb["xchunk"][:, slotid * 1024:(slotid + 1) * 1024]
                    mu = sb["stats"][:, c * NST + st:c * NST + st + 1]
                    rr = sb["stats"][:, NT + c * NST + st:
                                     NT + c * NST + st + 1]
                    rdy = stats_pass(xa, mu, rr, [toks[st]], c * NST + st)
                    mus.append((mu, rr, rdy, c * NST + st))
                ctx["mus"][c] = mus
            rts = []
            for t, k in ((0, "rope_c"), (1, "rope_s")):
                for st in range(NST):
                    t0 = c * CH + st * 128
                    da = sb["rope2"][:, par * 256 + t * 128 + st * 64:
                                     par * 256 + t * 128 + (st + 1) * 64]
                    rts.append(b.emit(
                        "scalar", lambda e, d=da, s=dr[k][t0:t0 + 128, :]:
                        e.dma_start(out=d, in_=s),
                        deps=[b.last(("rp", par))], inc=f"ldr{par}_{t}",
                        by=16))
            rld[c] = rts

        load_chunk(0)
        if NCH > 1:
            load_chunk(1)
        for c in range(NCH):
            par = c % 2
            mus = ctx["mus"][c]
            if mode == "B":
                kstore, vstore, snstore = ctx["kv"][c]
                kld = b.emit("scalar", lambda e, s_=dr["kvk"][
                    c * 128:(c + 1) * 128, :]:
                    e.dma_start(out=sb["kf"][:, :], in_=s_),
                    deps=[kstore] + [b.last(("kfh", hh)) for hh in range(H)],
                    inc="ldkf", by=16)
                kf_toks = [kld] * H
                for hh in range(H):
                    b.note(("kfh", hh), kld)
                vld = b.emit("scalar", lambda e, s_=dr["kvv"][
                    c * 128:(c + 1) * 128, :]:
                    e.dma_start(out=sb["vf"][:, :], in_=s_),
                    deps=[vstore] + [b.last(("vf", j)) for j in range(4)],
                    inc="ldvf", by=16)
                v_toks = [vld] * (NST * 2)
                for j in range(NST * 2):
                    b.note(("vf", j), vld)
                snld = b.emit("scalar", lambda e, s_=dr["snapb"][
                    c * 128:(c + 1) * 128, :]:
                    e.dma_start(out=snap, in_=s_),
                    deps=[snstore, b.last(("ktmj", 0)), b.last(("ktmj", 1)),
                          ("ldsn", b.cnt.get("ldsn", 0))],
                    inc="ldsn", by=16)
                scur_toks = []
                for h in range(H):
                    tk = b.emit("vector", lambda e,
                                o=sb["sst"][:, h * 128:(h + 1) * 128],
                                i=sin_m[:, h * 128:(h + 1) * 128],
                                gc=_gp(h, CH * c),
                                sn=snap[:, h * 128:(h + 1) * 128]:
                                e.scalar_tensor_tensor(o, i, gc, sn, ALU.mult,
                                                       ALU.add),
                                deps=[snld, ctx["sin_tok"],
                                      b.last("sst")], inc="ve")
                    scur_toks.append(tk)
                    b.note_add(("ktmj", 0), tk)
                    b.note_add(("ktmj", 1), tk)
                    b.note_add(("ktmj", 2), tk)
                    b.note_add(("ktmj", 3), tk)
                b.note("sst", scur_toks)
                kro = None
                vg_toks = None
                ka_toks = None
                vmm = [None] * (NST * 2)
                lnf_toks = ln_proj_fused(
                    sb["xchunk"], par * NST * 1024, mus, "lnf", [xld[c]],
                    [((f"wq{li}", kk, None), (f"wq{li}", kk + 1, None))
                     for kk in range(0, FC, 2)],
                    mm_tm(sb["lnf_h"], sb["lnf_l"], vmm),
                    pre_ln=ctx.get("pre_ln", {}).pop(c, None))
                q_toks = proj_evac(sb["qtm"], vmm, "qtmj",
                                   extra=[b.last(("ysb", hh))
                                          for hh in range(H)])
                lnf_all = b.emit("tensor", lambda e: e.nop(), deps=lnf_toks,
                                 inc="tnp")
            if mode != "B":
                vmm = [None] * (NST * 2)
                lnf_toks = ln_proj_fused(
                    sb["xchunk"], par * NST * 1024, mus, "lnf", [xld[c]],
                    [((f"wk{li}", kk, None), (f"wk{li}", kk + 1, None))
                     for kk in range(0, FC, 2)],
                    mm_tm(sb["lnf_h"], sb["lnf_l"], vmm),
                    pre_ln=ctx.get("pre_ln", {}).pop(c, None))
                k_toks = proj_evac(sb["ktm"], vmm, "ktmj")
                lnf_all = b.emit("tensor", lambda e: e.nop(), deps=lnf_toks,
                                 inc="tnp")
                v_toks = proj_tm(f"wv{li}", sb["vf"], lnf_all, "vf")
                kro = rope_tm(sb["ktm"], par, rld[c], k_toks, sign=-1)
                ka_toks = []
                for st in range(NST):
                    tk = b.emit("vector", lambda e,
                                o=sb["lnt"][:, st * 1024:(st + 1) * 1024],
                                i=sb["ktm"][:, st * 1024:(st + 1) * 1024],
                                g=sb["gka"][:, st * 1024:(st + 1) * 1024]:
                                e.tensor_mul(o, i, g),
                                deps=[CST, kro[st], b.last(("lnt", st))],
                                inc="ve")
                    ka_toks.append(tk)
                kf_toks = []
                for h in range(H):
                    bk = 4 + (h % 2)
                    tt = None
                    for st in range(NST):
                        ia = sb["lnt"][:, st * 1024 + h * 128:
                                       st * 1024 + (h + 1) * 128]
                        dd = [ka_toks[st], CST] + (b.bw(bk) if st == 0 else [])
                        tt = b.emit("tensor", lambda e,
                                    o=pb[bk][:, st * 128:(st + 1) * 128], i=ia:
                                    e.transpose(o, i, sb["ident"][:, :]),
                                    deps=dd, inc=("te" if st == NST - 1 else None))
                    b.set_w(bk, tt)
                    tk = b.emit("vector", lambda e,
                                o=sb["kf"][:, h * CH:(h + 1) * CH],
                                i=pb[bk][:, 0:NST * 128]: e.tensor_copy(o, i),
                                deps=[tt, b.last(("kfh", h))], inc="ve")
                    b.add_r(bk, tk)
                    b.note(("kfh", h), tk)
                    kf_toks.append(tk)
                for st in range(NST):
                    b.note(("lnt", st), kf_toks[-1])
                vg_toks = []
                for st in range(NST):
                    tk = b.emit("vector", lambda e,
                                o=sb["lnt"][:, st * 1024:(st + 1) * 1024],
                                i=sb["vf"][:, st * 1024:(st + 1) * 1024],
                                g=sb["gsf"][:, st * 1024:(st + 1) * 1024]:
                                e.tensor_mul(o, i, g),
                                deps=[CST, v_toks[2 * st], v_toks[2 * st + 1],
                                      b.last(("lnt", st))], inc="ve")
                    vg_toks.append(tk)
            if mode == "A":
                gsn = f"gsn{c % 2}"
                sn = b.emit("gpsimd", lambda e,
                            d=dr["snapb"][c * 128:(c + 1) * 128, :]:
                            e.dma_start(out=d, in_=sb["sst"][:, :]),
                            deps=[b.last("sst"),
                                  (gsn, b.cnt.get(gsn, 0))],
                            inc=gsn, by=16)
                ds_last = {}
                for grp in range(2):
                    bk = 2 + grp
                    first = True
                    for h in range(grp * 4, grp * 4 + 4):
                        for st in range(NST):
                            dd = [kro[st], vg_toks[st]]
                            if first:
                                dd += b.bw(bk)
                            ds_last[grp] = b.emit(
                                "tensor", lambda e, p_=pb[bk],
                                l=sb["ktm"][:, st * 1024 + h * 128:
                                            st * 1024 + (h + 1) * 128],
                                r=sb["lnt"][:, st * 1024 + h * 128:
                                            st * 1024 + (h + 1) * 128],
                                s=first, col=(h % 4) * 128,
                                sp=(h == grp * 4 + 3 and st == NST - 1):
                                e.matmul(p_[:, col:col + 128], l, r, start=s,
                                         stop=sp),
                                deps=dd,
                                inc=("te" if (h == grp * 4 + 3 and
                                              st == NST - 1) else None))
                            first = False
                    b.set_w(bk, ds_last[grp])
                su_toks = []
                for h in range(H):
                    bk = 2 + h // 4
                    tk = b.emit("vector", lambda e,
                                o=sb["sst"][:, h * 128:(h + 1) * 128],
                                p_=pb[bk][:, (h % 4) * 128:(h % 4 + 1) * 128],
                                gc=_gp(h, CH):
                                e.scalar_tensor_tensor(o, o, gc, p_, ALU.mult,
                                                       ALU.add),
                                deps=[ds_last[h // 4], sn, b.last("sst")],
                                inc="ve")
                    b.add_r(bk, tk)
                    su_toks.append(tk)
                b.note("sst", su_toks)
                gkf = f"gkf{c % 2}"
                kt = b.emit("gpsimd", lambda e,
                            d=dr["kvk"][c * 128:(c + 1) * 128, :]:
                            e.dma_start(out=d, in_=sb["kf"][:, :]),
                            deps=kf_toks + [(gkf, b.cnt.get(gkf, 0))],
                            inc=gkf, by=16)
                for hh in range(H):
                    b.note_add(("kfh", hh), kt)
                gvf = f"gvf{c % 2}"
                vt = b.emit("gpsimd", lambda e,
                            d=dr["kvv"][c * 128:(c + 1) * 128, :]:
                            e.dma_start(out=d, in_=sb["vf"][:, :]),
                            deps=v_toks + vg_toks +
                                 [(gvf, b.cnt.get(gvf, 0))],
                            inc=gvf, by=16)
                for j in range(NST * 2):
                    b.note_add(("vf", j), vt)
                    b.note(("ktmj", j), [ka_toks[j // 2], ds_last[0],
                                         ds_last[1]])
                ctx["kv"][c] = (kt, vt, sn)
                for fc in range(FC):
                    b.note_add(("fmh", "lnf", fc), [v_toks[-1], ds_last[1]])
                    b.note_add(("fml", "lnf", fc), [v_toks[-1], ds_last[1]])
                b.note(("rp", par), kro)
                for st in range(NST):
                    b.note(("xc", par * NST + st), lnf_toks[-1])
                if c + 1 < NCH:
                    ctx.setdefault("pre_ln", {})[c + 1] = emit_norm(
                        sb["xchunk"], ((c + 1) % 2) * NST * 1024,
                        ctx["mus"][c + 1],
                        [xld[c + 1], ds_last[0], ds_last[1]])
                if c + 2 < NCH:
                    load_chunk(c + 2)
                continue
            g_toks = proj_fm(f"wg{li}", sb["lnf"], lnf_all, sb["gfb"],
                             silu=True)
            for fc in range(FC):
                b.note_add(("fmh", "lnf", fc), g_toks[-1])
                b.note_add(("fml", "lnf", fc), g_toks[-1])
            qro = rope_tm(sb["qtm"], par, rld[c], q_toks, sign=+1)
            b.note(("rp", par), [kro, qro])
            qg_toks = []
            for st in range(NST):
                tk = b.emit("vector", lambda e,
                            o=sb["qtm"][:, st * 1024:(st + 1) * 1024],
                            g=sb["gq"][:, st * 1024:(st + 1) * 1024]:
                            e.tensor_mul(o, o, g),
                            deps=[CST, qro[st]], inc="ve")
                qg_toks.append(tk)
            qf_toks = []
            for h in range(H):
                bk = 4 + (h % 2)
                tt = None
                for st in range(NST):
                    ia = sb["qtm"][:, st * 1024 + h * 128:
                                   st * 1024 + (h + 1) * 128]
                    dd = [qg_toks[st], CST] + (b.bw(bk) if st == 0 else [])
                    tt = b.emit("tensor", lambda e,
                                o=pb[bk][:, st * 128:(st + 1) * 128], i=ia:
                                e.transpose(o, i, sb["ident"][:, :]),
                                deps=dd, inc=("te" if st == NST - 1 else None))
                b.set_w(bk, tt)
                tk = b.emit("vector", lambda e,
                            o=sb["qf"][:, h * CH:(h + 1) * CH],
                            i=pb[bk][:, 0:NST * 128]: e.tensor_copy(o, i),
                            deps=[tt, b.last(("qfh", h))], inc="ve")
                b.add_r(bk, tk)
                b.note(("qfh", h), tk)
                qf_toks.append(tk)
            ysb_toks = [None] * H
            a1_toks = [None] * H
            gst_last = None
            y2_last = None
            att_bk = (4, 0)
            y_bk = (5, 1)
            att_buf = (sb["attm"], sb["attm2"])

            def emit_att(h):
                bk = att_bk[h % 2]
                buf = att_buf[h % 2]
                am = b.last(("attm", h % 2))
                dd0 = [kf_toks[h], qf_toks[h]] + b.bw(bk)
                m0 = b.emit("tensor", lambda e, p_=pb[bk],
                            l=sb["kf"][:, h * CH:h * CH + 128],
                            r=sb["qf"][:, h * CH:(h + 1) * CH]:
                            e.matmul(p_[:, 0:CH], l, r, start=True,
                                     stop=False), deps=dd0)
                m1 = b.emit("tensor", lambda e, p_=pb[bk],
                            l=sb["kf"][:, h * CH + 128:h * CH + CH],
                            r=sb["qf"][:, h * CH + 128:(h + 1) * CH]:
                            e.matmul(p_[:, CH:CH + 128], l, r, start=False,
                                     stop=True), deps=[kf_toks[h], qf_toks[h]],
                            inc="te")
                b.set_w(bk, m1)
                e1 = b.emit("vector", lambda e, o=buf[:, 0:128],
                            p_=pb[bk]: e.tensor_mul(
                    o, p_[:, 0:128], sb["tri"][:, :]),
                    deps=[m1, CST, am], inc="ve")
                e2 = b.emit("vector", lambda e, o=buf[:, 128:256],
                            p_=pb[bk]: e.tensor_copy(o, p_[:, 128:256]),
                    deps=[m1, am], inc="ve")
                e3 = b.emit("vector", lambda e, o=buf[:, 256:384],
                            p_=pb[bk]: e.tensor_mul(
                    o, p_[:, 256:384], sb["tri"][:, :]),
                    deps=[m1, CST, am], inc="ve")
                for t in (e1, e2, e3):
                    b.add_r(bk, t)
                return (e1, e2, e3, buf)

            def emit_y(h, ectx):
                nonlocal y2_last
                e1, e2, e3, buf = ectx
                bk = y_bk[h % 2]
                dd = [b.last("sst"), qf_toks[h]] + b.bw(bk)
                y0 = b.emit("tensor", lambda e, p_=pb[bk],
                            l=sb["sst"][:, h * 128:(h + 1) * 128],
                            r=sb["qf"][:, h * CH:(h + 1) * CH]:
                            e.matmul(p_[:, 0:CH], l, r, start=True,
                                     stop=False), deps=dd)
                y1 = b.emit("tensor", lambda e, p_=pb[bk],
                            l=sb["vf"][:, h * 128:(h + 1) * 128],
                            r=buf[:, 0:CH]:
                            e.matmul(p_[:, 0:CH], l, r, start=False,
                                     stop=False),
                            deps=[e1, e2, v_toks[0], v_toks[1]])
                y2 = b.emit("tensor", lambda e, p_=pb[bk],
                            l=sb["vf"][:, 1024 + h * 128:1024 + (h + 1) * 128],
                            r=buf[:, 256:384]:
                            e.matmul(p_[:, 128:CH], l, r, start=False,
                                     stop=True),
                            deps=[e3, v_toks[2], v_toks[3]], inc="te")
                b.set_w(bk, y2)
                y2_last = y2
                b.note(("attm", h % 2), [e3, y2])
                ye = b.emit("scalar", lambda e, p_=pb[bk],
                            o=sb["ysb"][:, h * CH:(h + 1) * CH]:
                            e.copy(o, p_[:, 0:CH]),
                            deps=[y2, b.last(("ysb", h))], inc="sc")
                y2e = b.emit("scalar", lambda e, p_=pb[bk],
                             o=sb["fsc"][:, (h % 2) * CH:(h % 2 + 1) * CH]:
                             e.activation(o, p_[:, 0:CH], AF.Square),
                             deps=[y2, b.last("fsc"),
                                   b.last(("fsch", h % 2))], inc="sc")
                for t in (ye, y2e):
                    b.add_r(bk, t)
                ysb_toks[h] = ye
                return (ye, y2e)

            def emit_g(h, yctx):
                nonlocal gst_last
                ye, y2e = yctx
                dd = [ye, CST] + (b.bw(6) if h == 0 else [])
                g1 = b.emit("tensor", lambda e,
                            l=sb["head_sel"][:, h * 8:(h + 1) * 8],
                            r=sb["ysb"][:, h * CH:(h + 1) * CH],
                            s_=(h == 0):
                            e.matmul(pb[6][0:8, 0:CH], l, r, start=s_,
                                     stop=False), deps=dd)
                gst_last = b.emit("tensor", lambda e,
                                  l=sb["head_sel"][:, h * 8:(h + 1) * 8],
                                  r=sb["fsc"][:, (h % 2) * CH:
                                              (h % 2 + 1) * CH],
                                  sp_=(h == H - 1):
                                  e.matmul(pb[6][0:8, CH:2 * CH], l, r,
                                           start=False, stop=sp_),
                                  deps=[y2e, CST], inc="te")
                b.note(("fsch", h % 2), gst_last)
                b.note("fsc", gst_last)

            ectx_prev = None
            yctx_prev = None
            for h in range(H):
                cur = emit_att(h)
                if ectx_prev is not None:
                    ycur = emit_y(h - 1, ectx_prev)
                    if yctx_prev is not None:
                        emit_g(h - 2, yctx_prev)
                    yctx_prev = ycur
                ectx_prev = cur
            yctx_last = emit_y(H - 1, ectx_prev)
            emit_g(H - 2, yctx_prev)
            emit_g(H - 1, yctx_last)
            b.set_w(6, gst_last)
            g0 = sb["gstat"][0:8, 0:CH]
            g1_ = sb["gstat"][0:8, CH:2 * CH]
            g2 = sb["gstat"][0:8, 2 * CH:3 * CH]
            g3 = sb["gstat"][0:8, 3 * CH:4 * CH]
            gl = [b.last("gstat")] + [b.last(("hf", i)) for i in range(8)]
            s1_ = b.emit("vector", lambda e: e.tensor_scalar_mul(
                g0, pb[6][0:8, 0:CH], 1.0 / DH), deps=[gst_last, gl],
                inc="ve")
            s2_ = b.emit("vector", lambda e: e.tensor_scalar_mul(
                g1_, pb[6][0:8, CH:2 * CH], 1.0 / DH), deps=[gst_last, gl],
                inc="ve")
            for t in (s1_, s2_):
                b.add_r(6, t)
            s3_ = b.emit("vector", lambda e: e.tensor_tensor(
                g2, g0, g0, ALU.mult), deps=[s1_, gl], inc="ve")
            s4_ = b.emit("vector", lambda e: e.tensor_sub(g2, g1_, g2),
                         deps=[s2_, s3_], inc="ve")
            s5_ = b.emit("scalar", lambda e: e.activation(
                g2, g2, AF.Sqrt, bias=sb["epsc"][0:8, 0:1]),
                deps=[s4_, t_eps], inc="sc")
            s6_ = b.emit("vector", lambda e: e.reciprocal(g2, g2),
                         deps=[s5_], inc="ve")
            s7_ = b.emit("vector", lambda e: e.tensor_tensor(
                g3, g0, g2, ALU.mult), deps=[s6_, gl], inc="ve")
            gy_toks = []
            r2_last = None
            for h in range(H):
                bk = 4 + (h % 2)
                dd = [s6_, s7_, CST] + b.bw(bk)
                r1 = b.emit("tensor", lambda e,
                            l=sb["sel8"][0:8, h * 128:(h + 1) * 128], rr=g2,
                            p_=pb[bk][:, 0:CH]:
                            e.matmul(p_, l, rr, start=True,
                                     stop=False), deps=dd)
                r2 = b.emit("tensor", lambda e,
                            l=sb["sel8"][0:8, h * 128:(h + 1) * 128], rr=g3,
                            p_=pb[bk][:, CH:2 * CH]:
                            e.matmul(p_, l, rr, start=False,
                                     stop=True), deps=[s6_, s7_, CST],
                            inc="te")
                b.set_w(bk, r2)
                r2_last = r2
                a1 = b.emit("vector", lambda e,
                            o=sb["gyf"][:, h * CH:(h + 1) * CH],
                            i=sb["ysb"][:, h * CH:(h + 1) * CH],
                            p_=pb[bk][:, 0:CH]: e.tensor_mul(o, i, p_),
                            deps=[r2, ysb_toks[h], b.last(("gyf", h))],
                            inc="ve")
                a2 = b.emit("vector", lambda e,
                            o=sb["gyf"][:, h * CH:(h + 1) * CH],
                            p_=pb[bk][:, CH:2 * CH]: e.tensor_sub(o, o, p_),
                            deps=[a1], inc="ve")
                for t in (a1, a2):
                    b.add_r(bk, t)
                a3 = b.emit("vector", lambda e,
                            o=sb["gyf"][:, h * CH:(h + 1) * CH],
                            s=sb["gnsb"][:, li * 2 * FC + h:li * 2 * FC + h + 1],
                            bb=sb["gnsb"][:, li * 2 * FC + FC + h:
                                          li * 2 * FC + FC + h + 1]:
                            e.tensor_scalar(o, o, s, bb, ALU.mult, ALU.add),
                            deps=[a2, CST], inc="ve")
                a4 = b.emit("vector", lambda e,
                            o=sb["gyf"][:, h * CH:(h + 1) * CH],
                            g=sb["gfb"][:, h * CH:(h + 1) * CH]:
                            e.tensor_mul(o, o, g),
                            deps=[a3, g_toks[h // 2]], inc="ve")
                a1_toks[h] = a1
                b.note(("ysb", h), [a1, gst_last])
                b.note(("gyf", h), a4)
                gy_toks.append(a4)
            b.note("gstat", [s7_, r2_last])
            for i in range(8):
                b.note_add(("hf", i), [s7_, r2_last])
            for bk in range(4):
                b.note_add(("pf", id(sb["gfb"]), bk), gy_toks[-1])
            if mode == "merged":
                ds_last = {}
                for grp in range(2):
                    bk = 2 + grp
                    first = True
                    for h in range(grp * 4, grp * 4 + 4):
                        for st in range(NST):
                            dd = [kro[st], vg_toks[st]]
                            if first:
                                dd += b.bw(bk)
                            ds_last[grp] = b.emit(
                                "tensor", lambda e, p_=pb[bk],
                                l=sb["ktm"][:, st * 1024 + h * 128:
                                            st * 1024 + (h + 1) * 128],
                                r=sb["lnt"][:, st * 1024 + h * 128:
                                            st * 1024 + (h + 1) * 128],
                                s=first, col=(h % 4) * 128,
                                sp=(h == grp * 4 + 3 and st == NST - 1):
                                e.matmul(p_[:, col:col + 128], l, r, start=s,
                                         stop=sp),
                                deps=dd,
                                inc=("te" if (h == grp * 4 + 3 and st == NST - 1)
                                     else None))
                            first = False
                    b.set_w(bk, ds_last[grp])
                for j in range(NST * 2):
                    b.note(("ktmj", j), [ka_toks[j // 2], ds_last[0],
                                         ds_last[1]])
                    b.note(("qtmj", j), qf_toks[-1])
                    b.note_add(("vf", j), [vg_toks[j // 2], y2_last])
                for h in range(H):
                    b.note_add(("kfh", h), y2_last)
                    b.note_add(("qfh", h), y2_last)
                su_toks = []
                for h in range(H):
                    bk = 2 + h // 4
                    tk = b.emit("vector", lambda e,
                                o=sb["sst"][:, h * 128:(h + 1) * 128],
                                p_=pb[bk][:, (h % 4) * 128:(h % 4 + 1) * 128],
                                gc=_gp(h, CH):
                                e.scalar_tensor_tensor(o, o, gc, p_, ALU.mult,
                                                       ALU.add),
                                deps=[ds_last[h // 4], b.last("sst")], inc="ve")
                    b.add_r(bk, tk)
                    su_toks.append(tk)
                b.note("sst", su_toks)
            else:
                ds_last = {0: None, 1: None}
            gsp = []
            for half2 in range(2):
                sl = slice(half2 * 1024, (half2 + 1) * 1024)
                th = b.emit("scalar", lambda e, o=sb["gyf_h"][:, sl],
                            i=sb["gyf"][:, sl]: e.copy(o, i),
                            deps=[gy_toks, b.last(("gysp", half2)),
                                  b.last("qwt")], inc="sc")
                tl = b.emit("vector", lambda e, o=sb["gyf_l"][:, sl],
                            i=sb["gyf"][:, sl], h_=sb["gyf_h"][:, sl]:
                            e.tensor_sub(o, i, h_), deps=[th], inc="ve")
                b.note(("gysp", half2), [th, tl])
                b.note_add("qwtv", [th, tl])
                gsp.append([th, tl])
            gy_all = b.emit("tensor", lambda e: e.nop(), deps=gsp, inc="te")
            wo_mm = [None] * (NST * 2)
            wo_banks = [4, 5, 0, 1]
            for kk in range(0, FC, 2):
              ss_, wt2, wtok = wload_sp2((f"wo{li}", kk, None),
                                         (f"wo{li}", kk + 1, None))
              for dk in range(2):
                k = kk + dk
                wt = wt2[:, dk * 2048:(dk + 1) * 2048]
                for p, (xs, ws) in enumerate(((0, 0), (0, 1), (1, 0))):
                    lsrc = sb["gyf_h"] if xs == 0 else sb["gyf_l"]
                    for j in range(NST * 2):
                        st, half = divmod(j, 2)
                        bk = wo_banks[j]
                        dd = [wtok, gy_all]
                        if k == 0 and p == 0:
                            dd += b.bw(bk)
                        wo_mm[j] = b.emit(
                            "tensor", lambda e, p_=pb[bk],
                            l=lsrc[:, k * CH + st * 128:k * CH + (st + 1) * 128],
                            r=wt[:, ws * 1024 + half * 512:
                                 ws * 1024 + (half + 1) * 512],
                            s=(k == 0 and p == 0),
                            sp=(k == FC - 1 and p == 2):
                            e.matmul(p_[:, 0:512], l, r, start=s, stop=sp),
                            deps=dd,
                            inc=("te" if ((k == FC - 1 and p == 2)
                                          or (p == 2 and j == NST * 2 - 1))
                                 else None))
              wfree2(ss_, wo_mm[-1])
            for h in range(H):
                b.note_add(("gyf", h), wo_mm[-1])
            yr_toks = []
            for j in range(NST * 2):
                st, half = divmod(j, 2)
                bk = wo_banks[j]
                b.set_w(bk, wo_mm[j])
                slotid = par * NST + st
                xa = sb["xchunk"][:, slotid * 1024 + half * 512:
                                  slotid * 1024 + (half + 1) * 512]
                tk = b.emit("vector", lambda e,
                            o=sb["yr"][:, st * 1024 + half * 512:
                                       st * 1024 + (half + 1) * 512],
                            i=pb[bk][:, 0:512], x_=xa: e.tensor_add(o, i, x_),
                            deps=[wo_mm[j], xld[c][st], b.last(("yr", j))],
                            inc="ve")
                b.add_r(bk, tk)
                yr_toks.append(tk)
            for st in range(NST):
                b.note(("xc", par * NST + st),
                       [yr_toks[2 * st], yr_toks[2 * st + 1]])
            if c + 2 < NCH:
                load_chunk(c + 2)
            mus2 = []
            for st in range(NST):
                ya = sb["yr"][:, st * 1024:(st + 1) * 1024]
                mu = sb["st2"][:, st:st + 1]
                rr = sb["st2"][:, NST + st:NST + st + 1]
                rdy = stats_pass(ya, mu, rr,
                                 [yr_toks[2 * st], yr_toks[2 * st + 1]],
                                 ("i", st))
                mus2.append((mu, rr, rdy, ("i", st)))
            def mk_mm_w1(mm_bank, lastk):
                def mm_fc(fc, tlo, wt, wtok):
                    for p, (xs, ws) in enumerate(((0, 0), (0, 1), (1, 0))):
                        rsrc = sb["ln2f_h"] if xs == 0 else sb["ln2f_l"]
                        for j in range(8):
                            bk = j // 2
                            dd = [wtok, tlo]
                            if fc == 0 and p == 0 and j % 2 == 0:
                                dd += b.bw(bk)
                            want_tok = (fc == FC - 1 and p == 2 and j % 2 == 1)
                            mm = b.emit(
                                "tensor", lambda e,
                                p_=pb[bk][:, (j % 2) * CH:(j % 2 + 1) * CH],
                                l=wt[:, ws * 1024 + j * 128:
                                     ws * 1024 + (j + 1) * 128],
                                r=rsrc[:, fc * CH:(fc + 1) * CH],
                                s=(fc == 0 and p == 0 and j % 2 == 0),
                                sp=(fc == FC - 1 and p == 2 and j % 2 == 1):
                                e.matmul(p_, l, r, start=s, stop=sp),
                                deps=dd,
                                inc=("te" if (want_tok or (p == 2 and j == 7))
                                     else None))
                            if want_tok:
                                mm_bank[bk] = mm
                            if p == 2 and j == 7:
                                lastk[0] = mm
                    return lastk[0]
                return mm_fc

            mm_bank = [None] * 4
            lastk = [None]
            lnf2_toks = ln_proj_fused(
                sb["yr"], 0, mus2, "ln2f", [yr_toks],
                [((f"w1{li}", kk, 0), (f"w1{li}", kk + 1, 0))
                 for kk in range(0, FC, 2)],
                mk_mm_w1(mm_bank, lastk))
            ln2_all = b.emit("tensor", lambda e: e.nop(), deps=lnf2_toks,
                             inc="tnp")
            if mode == "B" and c + 1 < NCH:
                ctx.setdefault("pre_ln", {})[c + 1] = emit_norm(
                    sb["xchunk"], ((c + 1) % 2) * NST * 1024,
                    ctx["mus"][c + 1], [xld[c + 1], lnf2_toks[-1]])
            h_toks = []
            for rnd in range(2):
                if rnd == 1:
                  mm_bank = [None] * 4
                  for kk in range(0, FC, 2):
                    ss_, wt2, wtok = wload_sp2((f"w1{li}", kk, rnd),
                                               (f"w1{li}", kk + 1, rnd))
                    for dk in range(2):
                      k = kk + dk
                      wt = wt2[:, dk * 2048:(dk + 1) * 2048]
                      last_of_k = None
                      for p, (xs, ws) in enumerate(((0, 0), (0, 1), (1, 0))):
                        rsrc = sb["ln2f_h"] if xs == 0 else sb["ln2f_l"]
                        for j in range(8):
                            bk = j // 2
                            dd = [wtok, ln2_all]
                            if k == 0 and p == 0 and j % 2 == 0:
                                dd += b.bw(bk)
                            want_tok = (k == FC - 1 and p == 2 and j % 2 == 1)
                            mm = b.emit(
                                "tensor", lambda e,
                                p_=pb[bk][:, (j % 2) * CH:(j % 2 + 1) * CH],
                                l=wt[:, ws * 1024 + j * 128:
                                     ws * 1024 + (j + 1) * 128],
                                r=rsrc[:, k * CH:(k + 1) * CH],
                                s=(k == 0 and p == 0 and j % 2 == 0),
                                sp=(k == FC - 1 and p == 2 and j % 2 == 1):
                                e.matmul(p_, l, r, start=s, stop=sp),
                                deps=dd,
                                inc=("te" if (want_tok or (p == 2 and j == 7))
                                     else None))
                            if want_tok:
                                mm_bank[bk] = mm
                            if p == 2 and j == 7:
                                last_of_k = mm
                    wfree2(ss_, last_of_k)
                for bk in range(4):
                    b.set_w(bk, mm_bank[bk])
                    hcol = (rnd * 8 + bk * 2) * CH
                    t32 = b.emit("scalar", lambda e,
                                 o=sb["lnf"][:, bk * 512:(bk + 1) * 512],
                                 i=pb[bk][:, 0:512]:
                                 e.activation(o, i, AF.Gelu_apprx_tanh),
                                 deps=[mm_bank[bk], b.last(("g32", bk)),
                                       b.last(("fm32", bk * 2)),
                                       b.last(("fm32", bk * 2 + 1))],
                                 inc="sc")
                    b.note(("fm32", bk * 2), t32)
                    b.note(("fm32", bk * 2 + 1), t32)
                    b.add_r(bk, t32)
                    th = b.emit("scalar", lambda e,
                                o=sb["hf_h"][:, hcol:hcol + 512],
                                i=sb["lnf"][:, bk * 512:(bk + 1) * 512]:
                                e.copy(o, i),
                                deps=[t32, b.last(("hf", rnd * 4 + bk))],
                                inc="sc")
                    tl = b.emit("vector", lambda e,
                                o=sb["hf_l"][:, hcol:hcol + 512],
                                i=sb["lnf"][:, bk * 512:(bk + 1) * 512],
                                h_=sb["hf_h"][:, hcol:hcol + 512]:
                                e.tensor_sub(o, i, h_),
                                deps=[th], inc="ve")
                    b.note(("g32", bk), tl)
                    b.note(("hf", rnd * 4 + bk), [th, tl])
                    h_toks.append([th, tl])
            for fc in range(FC):
                b.note_add(("fmh", "ln2f", fc), h_toks[-1])
                b.note_add(("fml", "ln2f", fc), h_toks[-1])
            hf_all = b.emit("tensor", lambda e: e.nop(), deps=h_toks,
                            inc="tnp")
            w2_mm = [None] * (NST * 2)
            for kk in range(0, HC, 2):
              ss_, wt2, wtok = wload_sp2((f"w2{li}", kk, None),
                                         (f"w2{li}", kk + 1, None))
              for dk in range(2):
                k = kk + dk
                wt = wt2[:, dk * 2048:(dk + 1) * 2048]
                for p, (xs, ws) in enumerate(((0, 0), (0, 1), (1, 0))):
                    lsrc = sb["hf_h"] if xs == 0 else sb["hf_l"]
                    for j in range(NST * 2):
                        st, half = divmod(j, 2)
                        bk = wo_banks[j]
                        dd = [wtok, h_toks[(k // 8) * 4 + (k % 8) // 2]]
                        if k == 0 and p == 0:
                            dd += b.bw(bk)
                        w2_mm[j] = b.emit(
                            "tensor", lambda e, p_=pb[bk],
                            l=lsrc[:, k * CH + st * 128:k * CH + (st + 1) * 128],
                            r=wt[:, ws * 1024 + half * 512:
                                 ws * 1024 + (half + 1) * 512],
                            s=(k == 0 and p == 0),
                            sp=(k == HC - 1 and p == 2):
                            e.matmul(p_[:, 0:512], l, r, start=s, stop=sp),
                            deps=dd,
                            inc=("te" if ((k == HC - 1 and p == 2)
                                          or (p == 2 and j == NST * 2 - 1))
                                 else None))
              wfree2(ss_, w2_mm[-1])
            for i in range(8):
                b.note_add(("hf", i), w2_mm[-1])
            st_toks = {0: [], 1: []}
            for j in range(NST * 2):
                st, half = divmod(j, 2)
                bk = wo_banks[j]
                b.set_w(bk, w2_mm[j])
                tk = b.emit("vector", lambda e,
                            o=sb["ysb"][:, st * 1024 + half * 512:
                                        st * 1024 + (half + 1) * 512],
                            i=pb[bk][:, 0:512],
                            y_=sb["yr"][:, st * 1024 + half * 512:
                                        st * 1024 + (half + 1) * 512]:
                            e.tensor_add(o, i, y_),
                            deps=[w2_mm[j], yr_toks[j],
                                  b.last(("ysb", 2 * j)),
                                  b.last(("ysb", 2 * j + 1)),
                                  ds_last[0], ds_last[1]],
                            inc="ve")
                b.add_r(bk, tk)
                b.note(("yr", j), tk)
                st_toks[st].append(tk)
            for st in range(NST):
                t0 = c * CH + st * 128
                tt_i = c * NST + st
                gs_name = f"gst{tt_i % 4}"
                drain = [(gs_name, b.cnt.get(gs_name, 0))]
                tk = b.emit("gpsimd", lambda e,
                            d=dr[xout][t0:t0 + 128, :],
                            s=sb["ysb"][:, st * 1024:(st + 1) * 1024]:
                            e.dma_start(out=d, in_=s),
                            deps=st_toks[st] + war_marks(c * NST + st) + drain,
                            inc=gs_name, by=16)
                for hh in range(st * 4, st * 4 + 4):
                    b.note(("ysb", hh), tk)
                out_marks[c * NST + st] = tk
        return out_marks, xld

    def emit_state_exchange(ctx):
        sinm_v = sb["ktm"].ap()[:, 1024:2048]
        t1 = b.emit("vector", lambda e: e.tensor_scalar(
            sb["lnt"][:, 0:1024], sb["sst"][:, :], sb["cmask"][:, 0:1], None,
            ALU.mult), deps=[b.last("sst"), CST, b.last(("lnt", 0))],
            inc="ve")
        b.note(("lnt", 0), t1)
        ts_ = b.emit("gpsimd", lambda e: e.dma_start(
            out=dr["bst_i"][:, :], in_=sb["lnt"][:, 0:1024]),
            deps=[t1, b.last("gbst")], inc="gbst", by=16)
        if collective:
            groups = ([[i, i + 1] for i in range(0, n_cores, 2)]
                      if n_cores > 1 else [[0]])
            cc = b.emit("gpsimd", lambda e, g=groups: e.collective_compute(
                "AllReduce", ALU.add, replica_groups=g,
                ins=[dr["bst_i"].ap().opt()], outs=[dr["bst_o"].ap().opt()]),
                deps=[ts_], inc="cc")
            b.note("gbst", ts_)
        else:
            cc = b.emit("gpsimd", lambda e: e.dma_start(
                out=dr["bst_o"][:, :], in_=dr["bst_i"][:, :]),
                deps=[ts_], inc="gbst", by=16)
            b.note("gbst", cc)
        ld = b.emit("sync", lambda e: e.dma_start(
            out=sinm_v, in_=dr["bst_o"][:, :]),
            deps=[cc, b.last(("ktmj", 2)), b.last(("ktmj", 3)),
                  ("ldsm", b.cnt.get("ldsm", 0))],
            inc="ldsm", by=16)
        um = b.emit("vector", lambda e: e.tensor_scalar(
            sinm_v, sinm_v, sb["cmask"][:, 1:2], None, ALU.mult),
            deps=[ld, CST], inc="ve")
        b.note_add(("ktmj", 2), um)
        b.note_add(("ktmj", 3), um)
        b.note("sst", t1)
        ctx["sin_tok"] = um

    def emit_ln_pass(src_dram, dst_dram, rd_marks, war_marks):
        out_marks = {}
        for tt in range(NT):
            par = tt % 4
            xa = sb["xchunk"][:, par * 1024:(par + 1) * 1024]
            xtok = b.emit("sync", lambda e, d=xa,
                          s=dr[src_dram][tt * 128:(tt + 1) * 128, :]:
                          e.dma_start(out=d, in_=s),
                          deps=[b.last(("xc", par))] + rd_marks(tt),
                          inc=f"ldx{par}", by=16)
            mu = sb["stats"][:, tt:tt + 1]
            rr = sb["stats"][:, NT + tt:NT + tt + 1]
            ts_ = stats_pass(xa, mu, rr, [xtok], tt)
            tln = b.emit("vector", lambda e, i=xa, m=mu, r=rr: e.tensor_scalar(
                sb["lnt"][:, 0:1024], i, m, r, ALU.subtract, ALU.mult),
                deps=[ts_, xtok, b.last(("lnt", 0))], inc="ve")
            b.note(("xc", par), tln)
            b.note_add(("str", tt), tln)
            gs_name = f"gst{tt % 4}"
            drain = [(gs_name, b.cnt.get(gs_name, 0))]
            stok = b.emit("gpsimd", lambda e,
                          d=dr[dst_dram][tt * 128:(tt + 1) * 128, :],
                          s=sb["lnt"][:, 0:1024]: e.dma_start(out=d, in_=s),
                          deps=[tln] + war_marks(tt) + drain,
                          inc=gs_name, by=16)
            b.note(("lnt", 0), stok)
            out_marks[tt] = stok
        return out_marks

    def emit_merge(srcA, srcB, dst_dram, rdA, rdB, war_marks):
        out_marks = {}
        for tt in range(NT):
            par = tt % 2
            xa = sb["xchunk"][:, par * 1024:(par + 1) * 1024]
            xbslot = 2 + par
            xbv = sb["xchunk"][:, xbslot * 1024:(xbslot + 1) * 1024]
            atok = b.emit("sync", lambda e, d=xa,
                          s=dr[srcA][tt * 128:(tt + 1) * 128, :]:
                          e.dma_start(out=d, in_=s),
                          deps=[b.last(("xc", par))] + rdA(tt),
                          inc=f"ldx{par}", by=16)
            btok = b.emit("sync", lambda e, d=xbv,
                          s=dr[srcB][tt * 128:(tt + 1) * 128, :]:
                          e.dma_start(out=d, in_=s),
                          deps=[b.last(("xc", xbslot))] + rdB(tt),
                          inc=f"ldx{xbslot}", by=16)
            tadd = b.emit("vector", lambda e, a=xa, bb=xbv:
                          e.tensor_add(sb["lnt"][:, 0:1024], a, bb),
                          deps=[atok, btok, b.last(("lnt", 0))], inc="ve")
            b.note(("xc", par), tadd)
            b.note(("xc", xbslot), tadd)
            gs_name = f"gst{tt % 4}"
            drain = [(gs_name, b.cnt.get(gs_name, 0))]
            stok = b.emit("gpsimd", lambda e,
                          d=dr[dst_dram][tt * 128:(tt + 1) * 128, :],
                          s=sb["lnt"][:, 0:1024]: e.dma_start(out=d, in_=s),
                          deps=[tadd] + war_marks(tt) + drain,
                          inc=gs_name, by=16)
            b.note(("lnt", 0), stok)
            out_marks[tt] = stok
        return out_marks

    wsg = emit_wsplit_gen()
    qw_tok, ws_tok = emit_qw_prep()
    for _ in wsg:
        pass

    def mk_rd(mk):
        return lambda tt: [mk[tt]] if tt in mk else []

    def mk_war(reads):
        rmap = {}
        for cc, toks in reads.items():
            for st_, tok in enumerate(toks):
                rmap[cc * NST + st_] = tok
        return lambda tt: [rmap[tt]] if tt in rmap else []

    none_m = lambda tt: []
    m_x1 = emit_ffn_outer("x_in", "xb0", qw_tok, ws_tok, none_m, none_m)
    b.note("qwt", [m_x1[NT - 1], ("qst", b.cnt.get("qst", 0))])
    m_s0 = emit_ln_pass("xb0", "xb1", mk_rd(m_x1), none_m)
    marks = m_s0
    reads = {}
    war_prev = none_m
    buf_in, buf_out = "xb1", "xb2"
    for li in range(depth):
        if seq_split:
            ctx = {}
            emit_layer(li, buf_in, None, mk_rd(marks), none_m, mode="A",
                       ctx=ctx)
            emit_state_exchange(ctx)
            marks, xld = emit_layer(li, buf_in, buf_out, mk_rd(marks),
                                    war_prev, mode="B", ctx=ctx)
        else:
            marks, xld = emit_layer(li, buf_in, buf_out, mk_rd(marks),
                                    war_prev)
        war_prev = mk_war(xld)
        reads = xld
        buf_in, buf_out = buf_out, buf_in
    m_x2 = emit_merge("xb0", buf_in, buf_out, mk_rd(m_x1), mk_rd(marks),
                      war_prev)
    emit_ffn_outer(buf_out, "out", None, ws_tok, mk_rd(m_x2), none_m,
                   qw_loader=emit_qw_reload)
    fin = [(f"gst{i}", b.cnt.get(f"gst{i}", 0)) for i in range(4)
           if b.cnt.get(f"gst{i}", 0) > 0]
    if b.cnt.get("gbc", 0):
        fin.append(("gbc", b.cnt["gbc"]))
    b.wait("gpsimd", *fin)

    with nc.Block() as block:
        @block.sync
        def _(e):
            for f in b.stream["sync"]:
                f(e)

        @block.scalar
        def _(e):
            for f in b.stream["scalar"]:
                f(e)

        @block.vector
        def _(e):
            for f in b.stream["vector"]:
                f(e)

        @block.tensor
        def _(e):
            for f in b.stream["tensor"]:
                f(e)

        @block.gpsimd
        def _(e):
            for f in b.stream["gpsimd"]:
                f(e)

    return nc


def _prep_inputs(inputs, T, depth, n_cores=8, seq_split=False):
    perm = np.zeros(D, dtype=np.int64)
    for h in range(H):
        base = h * DH
        perm[base:base + 64] = base + np.arange(0, DH, 2)
        perm[base + 64:base + DH] = base + np.arange(1, DH, 2)
    consts = host_constants()
    cos_t, sin_t = rope_tables(0, T)
    ln_sb = np.concatenate(
        [np.asarray(inputs["L0_attn_ln_scale"], np.float32).reshape(FC, 128).T,
         np.asarray(inputs["L0_attn_ln_bias"], np.float32).reshape(FC, 128).T],
        axis=1)
    gn = np.asarray(inputs["L0_attn_gn_scale"], np.float32)
    gnb = np.asarray(inputs["L0_attn_gn_bias"], np.float32)
    gn_sb = np.concatenate(
        [np.concatenate([gn[li].reshape(FC, 128).T,
                         gnb[li].reshape(FC, 128).T], axis=1)
         for li in range(depth)], axis=1)
    fbias = np.broadcast_to(
        np.asarray(inputs["L0_ffn_bias"], np.float32)[None, :], (128, D)).copy()
    shared = {
        "fw": np.ascontiguousarray(np.asarray(inputs["L0_ffn_weight"],
                                              np.float32)),
        "fbias": fbias, "ln_sb": np.ascontiguousarray(ln_sb),
        "gn_sb": np.ascontiguousarray(gn_sb),
        "rope_c": cos_t, "rope_s": sin_t,
        "gq_t": consts["gq_t"], "gka_t": consts["gka_t"],
        "gsf_t": consts["gsf_t"], "tri": consts["tri"],
        "ident": consts["ident"], "ones1": consts["ones1"],
        "sel8": consts["sel8"], "head_sel": consts["head_sel"],
    }
    for li in range(depth):
        shared[f"wq{li}"] = np.ascontiguousarray(
            np.asarray(inputs["L0_attn_Wq"][li], np.float32)[:, perm])
        shared[f"wk{li}"] = np.ascontiguousarray(
            np.asarray(inputs["L0_attn_Wk"][li], np.float32)[:, perm])
        shared[f"wv{li}"] = np.ascontiguousarray(
            np.asarray(inputs["L0_attn_Wv"][li], np.float32))
        shared[f"wg{li}"] = np.ascontiguousarray(
            np.asarray(inputs["L0_attn_Wg"][li], np.float32))
        shared[f"wo{li}"] = np.ascontiguousarray(
            np.asarray(inputs["L0_attn_Wo"][li], np.float32))
        shared[f"w1{li}"] = np.ascontiguousarray(
            np.asarray(inputs["L0_attn_W1"][li], np.float32))
        shared[f"w2{li}"] = np.ascontiguousarray(
            np.asarray(inputs["L0_attn_W2"][li], np.float32))
    X = np.asarray(inputs["L0_attn_X"], np.float32)
    in_maps = []
    for c in range(n_cores):
        m = dict(shared)
        if seq_split:
            u = c % 2
            m["x_in"] = np.ascontiguousarray(X[c // 2, u * T:(u + 1) * T])
            cs, sn = rope_tables(u * T, T)
            m["rope_c"], m["rope_s"] = cs, sn
            cmask = np.zeros((128, 2), np.float32)
            cmask[:, 0] = 1.0 if u == 0 else 0.0
            cmask[:, 1] = 0.0 if u == 0 else 1.0
            m["cmask"] = cmask
        else:
            m["x_in"] = np.ascontiguousarray(X[c // 2])
            m["cmask"] = np.zeros((128, 2), np.float32)
        in_maps.append(m)
    return in_maps


def kernel(**inputs):
    T = 4096
    nc = build_program(T, depth=DEPTH, n_cores=8, collective=True,
                       seq_split=True)
    in_maps = _prep_inputs(inputs, T, DEPTH, seq_split=True)
    res = run_bass_kernel_spmd(nc, in_maps, core_ids=list(range(8)))
    out = np.stack(
        [np.concatenate([res.results[2 * bq]["out"],
                         res.results[2 * bq + 1]["out"]], axis=0)
         for bq in range(4)], axis=0)
    return np.ascontiguousarray(out.astype(np.float32))
